# version 07
# baseline (speedup 1.0000x reference)
import numpy as np
import concourse.bass as bass
import concourse.mybir as mybir
from concourse.bass_utils import run_bass_kernel_spmd
from concourse.alu_op_type import AluOpType as OP
from contextlib import ExitStack

F32 = mybir.dt.float32
BF16 = mybir.dt.bfloat16
I32 = mybir.dt.int32
AF = mybir.ActivationFunctionType

NB, S, D = 2, 2048, 1024
NH, DH, NPAIR = 16, 64, 8
KEEP = 204
FH, NE = 2048, 8
NCH, NT, NBLK = 8, 16, 4
T0, STEP0, NEG = 1.30, 0.45, -1.0e30
ZINIT, SSTEP = 1.25, 0.40
RPROBE = 5
DVE_ROUNDS = ()
SRING = 11
SRING_XG = (1, 2, 3, 5, 6, 7)
GW = 4

XG_E = 0
XG_G = 4096
XG_IND = 8192
XG_EM = 10240
XG_ETH = 12288


class B:
    pass


def inc(b, instr, name, n=1):
    s = b.sems[name]
    instr.then_inc(s[0], n)
    s[1] += n
    return s[1]


def cnt(b, name):
    return b.sems[name][1]


def dinc(b, eng, name, n=1):
    s = b.sems[name]
    eng.maybe_drain_then_inc((s[0], n))
    s[1] += n
    return s[1]


def wt(b, eng, name, val=None):
    if val is None:
        val = b.sems[name][1]
    if val > 0:
        eng.wait_ge(b.sems[name][0], val)


def vg(b, instr):
    n = inc(b, instr, "vraw", 1)
    b.nc.vector.wait_ge(b.sems["vraw"][0], n)


def barrier(b):
    nc = b.nc
    b.phase += 1
    pairs = [(nc.tensor, "bar_t"), (nc.vector, "bar_v"), (nc.scalar, "bar_a"),
             (nc.gpsimd, "bar_g"), (nc.sync, "bar_y")]
    for eng, nm in pairs:
        dinc(b, eng, nm, 1)
    for eng, nm in pairs:
        for _, nm2 in pairs:
            if nm2 != nm:
                eng.wait_ge(b.sems[nm2][0], b.phase)


def xslot(b, c):
    if c < 4:
        return b.xf[:, c * 2048:(c + 1) * 2048]
    return b.xg[:, (c - 4) * 4096:(c - 4) * 4096 + 4096].bitcast(F32)


def hslot(b, c):
    if c < 4:
        return b.xf[:, c * 2048:c * 2048 + 1024].bitcast(BF16)
    return b.xg[:, (c - 4) * 4096:(c - 4) * 4096 + 2048]


def xgsl(b, off, n):
    return b.xg[:, off:off + n]


def onorm(b, c):
    return b.xf[:, c * 1024:(c + 1) * 1024].bitcast(BF16)


def mixf(b, tt):
    if tt < 8:
        return b.xf[:, tt * 1024:(tt + 1) * 1024]
    return b.xg[:, (tt - 8) * 2048:(tt - 8) * 2048 + 2048].bitcast(F32)


def qbf(b, off, n):
    return b.qb[:, 2 * off:2 * off + 2 * n].bitcast(F32)


def kbf(b, off, n):
    return b.kb[:, 2 * off:2 * off + 2 * n].bitcast(F32)


def srf(b, off, n):
    return b.sr[:, 2 * off:2 * off + 2 * n].bitcast(F32)


def ld3(dst2d, srcdr, inner):
    return (dst2d.rearrange("p (c o) -> p c o", o=inner),
            srcdr.rearrange("(c p) o -> p c o", p=128))


def lnsc(b):
    return srf(b, 0, 2048)


def statsc(b, off, n):
    return b.sr[:, 4096 + off:4096 + off + n]


def wcolv(b, slot):
    return b.sr[:, 4096 + slot * 1024:4096 + slot * 1024 + 1024]


def gbbc(b):
    return srf(b, 2048, 1024)


def otsv(b, n=1024):
    return srf(b, 3072, n)


def build(debug=False):
    b = B()
    b.debug = debug
    nc = bass.Bass("TRN2", target_bir_lowering=False, detect_race_conditions=False)
    b.nc = nc
    ctx = ExitStack()
    b.ctx = ctx
    E = ctx.enter_context

    dri = lambda n, sh: nc.dram_tensor(n, sh, F32, kind="ExternalInput").ap()
    dro = lambda n, sh: nc.dram_tensor(n, sh, F32, kind="ExternalOutput").ap()

    def drt(n, sh):
        kind = "ExternalOutput" if debug else "Internal"
        return nc.dram_tensor(n, sh, F32, kind=kind).ap()

    b.x = dri("x", [NB, S, D])
    b.wq = dri("wq", [D, D]); b.wk = dri("wk", [D, D])
    b.wv = dri("wv", [D, D]); b.wo = dri("wo", [D, D])
    b.bq = dri("bq", [D]); b.bk = dri("bk", [D])
    b.bv = dri("bv", [D]); b.bo = dri("bo", [D])
    b.rwt = dri("rwt", [D, NE]); b.rbt = dri("rbt", [NE])
    b.sw1 = dri("sw1", [D, FH]); b.sb1 = dri("sb1", [FH])
    b.sw2 = dri("sw2", [FH, D]); b.sb2 = dri("sb2", [D])
    b.rw1 = dri("rw1", [NE, D, FH]); b.rb1 = dri("rb1", [NE, FH])
    b.rw2 = dri("rw2", [NE, FH, D]); b.rb2 = dri("rb2", [NE, D])
    b.cc = dri("cc", [NB, D]); b.wmod = dri("wmod", [D, 6 * D])
    b.out = dro("out", [NB, S, D])

    b.spillA = drt("spillA", [NCH, 128, S])
    b.spillB = drt("spillB", [NCH, 128, S])
    b.ethd = nc.dram_tensor("ethd", [2, S], F32, kind="Internal").ap()
    b.rowd = nc.dram_tensor("rowd", [4, S], F32, kind="Internal").ap()
    if debug:
        b.dbg_h = dro("dbg_h", [NCH, 128, S])
        b.dbg_h2 = dro("dbg_h2", [NCH, 128, S])
        b.dbg_ts = dro("dbg_ts", [NB, NPAIR, 128, 32])
        b.dbg_mod = dro("dbg_mod", [NB, 128, 48])
        b.dbg_w = dro("dbg_w", [128, 128])
        b.dbg_mx0 = dro("dbg_mx0", [NT, 128, 1024])
        b.dbg_mx1 = dro("dbg_mx1", [NT, 128, 1024])
        b.dbg_q = dro("dbg_q", [128, S])
        b.dbg_k = dro("dbg_k", [128, S])
        b.dbg_v = dro("dbg_v", [128, 2080])

    b.xf = E(nc.sbuf_tensor("xf", [128, 8192], F32))
    b.xg = E(nc.sbuf_tensor("xg", [128, 16384], BF16))
    b.rwr = E(nc.sbuf_tensor("rwr", [1, 2048], F32))
    b.qb = E(nc.sbuf_tensor("qb", [128, 16384], BF16))
    b.kb = E(nc.sbuf_tensor("kb", [128, 16384], BF16))
    b.vb = E(nc.sbuf_tensor("vb", [128, 16640], BF16))
    b.sr = E(nc.sbuf_tensor("sr", [128, 11264], BF16))
    b.identf = E(nc.sbuf_tensor("identf", [128, 128], F32))
    b.io32 = E(nc.sbuf_tensor("io32", [128, 128], I32))
    b.onesr = E(nc.sbuf_tensor("onesr", [1, 512], BF16))
    b.onesc = E(nc.sbuf_tensor("onesc", [128, 1], BF16))
    b.modv = E(nc.sbuf_tensor("modv", [128, 48], F32))
    b.csil = E(nc.sbuf_tensor("csil", [128, 8], BF16))
    b.craw = E(nc.sbuf_tensor("craw", [128, 8], F32))
    b.bqs = E(nc.sbuf_tensor("bqs", [128, 16], F32))
    b.bvrow = E(nc.sbuf_tensor("bvrow", [1, 1024], BF16))
    b.borow = E(nc.sbuf_tensor("borow", [1, 1024], BF16))
    b.b2row = E(nc.sbuf_tensor("b2row", [2, 1024], BF16))
    b.rbrow = E(nc.sbuf_tensor("rbrow", [1, 8], BF16))
    b.b1ch = E(nc.sbuf_tensor("b1ch", [128, 32], F32))
    b.rwtb = E(nc.sbuf_tensor("rwtb", [128, 64], BF16))
    for nm in ("lo", "hi", "chi", "clo", "pr", "cn", "ts", "s1", "s2", "s3",
               "sm", "sq", "sd", "u1", "u2"):
        setattr(b, "st_" + nm, E(nc.sbuf_tensor("st_" + nm, [128, 32], F32)))
    b.w9 = E(nc.sbuf_tensor("w9", [128, 448], F32))
    b.iota9 = E(nc.sbuf_tensor("iota9", [128, 9], F32))
    b.vsums = E(nc.sbuf_tensor("vsums", [65, 16], F32))
    b.eth = E(nc.sbuf_tensor("eth", [128, 32], F32))
    b.ethT = E(nc.sbuf_tensor("ethT", [32, 128], F32))
    b.rzb = E(nc.sbuf_tensor("rzb", [1, 512], BF16))
    b.gbt = E(nc.sbuf_tensor("gbt", [8, 128], F32))
    b.o1 = E(nc.sbuf_tensor("o1", [64, 512], F32))
    b.wall = E(nc.sbuf_tensor("wall", [128, 128], F32))
    b.prsc = E(nc.sbuf_tensor("prsc", [128, 24], F32))
    b.cst = E(nc.sbuf_tensor("cst", [128, 4], F32))

    b.psA = E(nc.psum_tensor("psA", [128, 2048], F32))
    b.psB = E(nc.psum_tensor("psB", [128, 2048], F32))

    b.sems = {}
    for name in ("bar_t", "bar_v", "bar_a", "bar_g", "bar_y",
                 "d_x", "d_xa", "d_xb", "d_bq", "d_w", "d_wma", "d_wmb", "d_w1", "d_w2",
                 "d_sp", "d_eth", "d_out", "d_row", "d_bc", "d_bce", "d_x2", "d_x2a", "d_x2b",
                 "tmm", "aev", "vev", "sfree", "upd", "prd", "pra",
                 "p2e", "p2g", "resid", "ethx",
                 "ff1", "ff2", "ffg", "ffm", "aux1", "aux2", "aux3", "aux4", "vraw"):
        b.sems[name] = [E(nc.semaphore(name)), 0]
    b.phase = 0
    b.guctr = 0
    b.sfree_post = {}
    b.block = E(nc.Block())
    return b


def emit(b):
    nc = b.nc
    T, V, A, G, Y = nc.tensor, nc.vector, nc.scalar, nc.gpsimd, nc.sync

    d = G.iota(b.io32[:], pattern=[[1, 128]], base=0, channel_multiplier=-1)
    inc(b, d, "aux1", 1)
    iv = b.w9[:, 0:9].bitcast(I32)
    d = G.iota(iv, pattern=[[1, 9]], base=0, channel_multiplier=0)
    inc(b, d, "aux1", 1)
    wt(b, V, "aux1", 2)
    V.tensor_scalar(out=b.identf[:], in0=b.io32[:], scalar1=0.0, scalar2=None, op0=OP.is_equal)
    V.tensor_copy(b.iota9[:], iv)
    V.memset(b.onesr[:], 1.0)
    V.memset(b.onesc[:], 1.0)
    V.memset(b.cst[:, 0:1], 1e-5)
    dinc(b, V, "aux2", 1)

    d = G.dma_start(b.bvrow[:], b.bv[None, :]); inc(b, d, "d_w", 16)
    d = G.dma_start(b.borow[:], b.bo[None, :]); inc(b, d, "d_w", 16)
    d = G.dma_start(b.rbrow[:], b.rbt[None, :]); inc(b, d, "d_w", 16)
    _d, _s = ld3(b.rwtb[:], b.rwt, NE)
    d = G.dma_start(_d, _s); inc(b, d, "d_w", 16)
    d = Y.dma_start(b.bqs[:, 0:8], b.bq.rearrange("(c p) -> p c", p=128), allow_slow_non_contiguous=True); inc(b, d, "d_bq", 16)
    d = Y.dma_start(b.bqs[:, 8:16], b.bk.rearrange("(c p) -> p c", p=128), allow_slow_non_contiguous=True); inc(b, d, "d_bq", 16)

    import os
    stage = int(os.environ.get("KSTAGE", "8"))
    phases = [p0_load_mod, p1_ln1, p2_qkv, p3_attention, p4_ln2, p5_router, p6_ffn, p7_out]
    for bi in range(NB):
        for pi, ph in enumerate(phases[:stage]):
            ph(b, bi)

    wt(b, Y, "d_out")


def dump(b, src_f32_ap, dst_dram_ap):
    nc = b.nc
    V, Y = nc.vector, nc.sync
    d = Y.dma_start(dst_dram_ap, src_f32_ap)
    n = inc(b, d, "d_out", 16)
    wt(b, Y, "d_out", n)


def p0_load_mod(b, bi):
    nc = b.nc
    T, V, A, G, Y = nc.tensor, nc.vector, nc.scalar, nc.gpsimd, nc.sync
    barrier(b)

    d = Y.dma_start(b.craw[:], b.cc[bi].rearrange("(c p) -> p c", p=128), allow_slow_non_contiguous=True)
    inc(b, d, "d_row", 16)
    wt(b, A, "d_row")
    A.activation(out=b.craw[:], in_=b.craw[:], func=AF.Silu)
    dinc(b, A, "aux3", 1)
    wt(b, V, "aux3")
    V.tensor_copy(b.csil[:], b.craw[:])
    dinc(b, V, "aux3", 1)

    wt(b, T, "aux2", 1)
    tposted = []
    for t in range(NT):
        buf = srf(b, (t % 2) * 1024, 1024)
        if t >= 2:
            wt(b, Y, "tmm", tposted[t - 2])
        xsem = "d_xa" if (t % 2) == 0 else "d_xb"
        d = Y.dma_start(buf, b.x[bi, t * 128:(t + 1) * 128, :])
        nld = inc(b, d, xsem, 16)
        wt(b, T, xsem, nld)
        if t >= 2:
            wt(b, T, "aev", cnt(b, "aev") - NCH)
        ps = b.psA if (t % 2) == 0 else b.psB
        for c in range(NCH):
            tr = T.transpose(ps[:, c * 128:(c + 1) * 128], buf[:, c * 128:(c + 1) * 128],
                             b.identf[:])
            inc(b, tr, "tmm", 1)
        tposted.append(cnt(b, "tmm"))
        wt(b, A, "tmm")
        wt(b, V, "tmm")
        for c in range(NCH):
            dstx = xslot(b, c)[:, t * 128:(t + 1) * 128]
            if c < 4:
                ev = A.activation(out=dstx, in_=ps[:, c * 128:(c + 1) * 128], func=AF.Copy)
            else:
                ev = V.tensor_copy(dstx, ps[:, c * 128:(c + 1) * 128])
            inc(b, ev, "aev", 1)

    wt(b, T, "aev")
    wt(b, T, "aux3", cnt(b, "aux3"))
    wcons = []
    for oc in range(48):
        w = wcolv(b, oc % 2)
        if oc >= 2:
            wt(b, G, "tmm", wcons[oc - 2])
        _d, _s = ld3(w, b.wmod[:, oc * 128:(oc + 1) * 128], 128)
        d = G.dma_start(_d, _s)
        wsem = "d_wma" if (oc % 2) == 0 else "d_wmb"
        ndw = inc(b, d, wsem, 16)
        wt(b, T, wsem, ndw)
        for kc in range(NCH):
            mm = T.matmul(b.psA[:, 1024 + oc:1024 + oc + 1], w[:, kc * 128:(kc + 1) * 128],
                          b.csil[:, kc:kc + 1], start=(kc == 0), stop=(kc == NCH - 1),
                          skip_group_check=True)
        inc(b, mm, "tmm", 1)
        wcons.append(cnt(b, "tmm"))
    wt(b, A, "tmm")
    ev = A.activation(out=b.modv[:], in_=b.psA[:, 1024:1072], func=AF.Copy)
    inc(b, ev, "aev", 1)
    wt(b, V, "aev")
    V.tensor_scalar(out=b.modv[:, 8:16], in0=b.modv[:, 8:16], scalar1=1.0, scalar2=None, op0=OP.add)
    V.tensor_scalar(out=b.modv[:, 32:40], in0=b.modv[:, 32:40], scalar1=1.0, scalar2=None, op0=OP.add)
    for col in (16, 40):
        sl = b.modv[:, col:col + 8]
        p = V.tensor_scalar(out=b.st_s1[:, 0:8], in0=sl, scalar1=0.0, scalar2=2.0,
                            op0=OP.is_ge, op1=OP.mult)
        vg(b, p)
        p = V.tensor_scalar(out=b.st_s1[:, 0:8], in0=b.st_s1[:, 0:8], scalar1=-1.0,
                            scalar2=None, op0=OP.add)
        p = V.tensor_scalar(out=b.st_s2[:, 0:8], in0=sl, scalar1=-1.0, scalar2=None,
                            op0=OP.mult)
        vg(b, p)
        p = V.tensor_tensor(out=b.st_s2[:, 0:8], in0=sl, in1=b.st_s2[:, 0:8], op=OP.max)
        vg(b, p)
        p = V.tensor_scalar(out=b.st_s2[:, 0:8], in0=b.st_s2[:, 0:8], scalar1=0.25,
                            scalar2=None, op0=OP.max)
        vg(b, p)
        p = V.tensor_tensor(out=sl, in0=b.st_s1[:, 0:8], in1=b.st_s2[:, 0:8], op=OP.mult)
        vg(b, p)
    dinc(b, V, "aux4", 1)

    wt(b, T, "aux4", cnt(b, "aux4"))
    tr = T.transpose(b.psB[0:8, 0:128], b.modv[:, 40:48], b.identf[:])
    ntm = inc(b, tr, "tmm", 1)
    wt(b, V, "tmm", ntm)
    V.tensor_copy(b.gbt[:], b.psB[0:8, 0:128])
    nvv = dinc(b, V, "vev", 1)
    wt(b, Y, "vev", nvv)
    d = Y.dma_start(b.rowd[3:4, 0:1024].rearrange("a (c q) -> (a c) q", q=128), b.gbt[:])
    ndr = inc(b, d, "d_row", 16)
    wt(b, Y, "d_row", ndr)
    if b.debug:
        dump(b, b.modv[:], b.dbg_mod[bi, :, :])


def ln_phase(b, bi, src_resident, bchome, ca_col, sa_col, dst_xb, spill):
    nc = b.nc
    T, V, A, G, Y = nc.tensor, nc.vector, nc.scalar, nc.gpsimd, nc.sync
    home = qbf if bchome == "qb" else kbf
    stream = not src_resident
    vdone = []

    for c in range(NCH):
        if stream:
            xcb = b.xf[:, (c % 2) * 2048:(c % 2) * 2048 + 2048]
            x2sem = "d_x2a" if (c % 2) == 0 else "d_x2b"
            if c >= 2:
                wt(b, Y, "vev", vdone[c - 2])
            d = Y.dma_start(xcb, b.spillB[c])
            nld = inc(b, d, x2sem, 16)
            wt(b, V, x2sem, nld)
        else:
            xcb = xslot(b, c)
        V.tensor_copy(statsc(b, 0, 2048), xcb)
        V.tensor_tensor(out=statsc(b, 2048, 2048), in0=xcb, in1=xcb, op=OP.mult)
        nvv = dinc(b, V, "vev", 1)
        vdone.append(nvv)
        wt(b, T, "vev", nvv)
        for blk in range(NBLK):
            mm = T.matmul(b.psA[0:1, blk * 512:(blk + 1) * 512], b.onesc[:],
                          statsc(b, blk * 512, 512),
                          start=(c == 0), stop=(c == NCH - 1), skip_group_check=True)
            mm = T.matmul(b.psB[0:1, blk * 512:(blk + 1) * 512], b.onesc[:],
                          statsc(b, 2048 + blk * 512, 512),
                          start=(c == 0), stop=(c == NCH - 1), skip_group_check=True)
        inc(b, mm, "tmm", 1)

    wt(b, V, "tmm")
    L1 = lnsc(b)[0:1, :]
    L2 = srf(b, 3072, 2048)[0:1, :]
    V.tensor_scalar(out=L1, in0=b.psA[0:1, :], scalar1=1.0 / D, scalar2=None, op0=OP.mult)
    V.tensor_scalar(out=L2, in0=b.psB[0:1, :], scalar1=1.0 / D, scalar2=None, op0=OP.mult)
    nvv = dinc(b, V, "vev", 1)
    wt(b, Y, "vev", nvv)
    d = Y.dma_start(b.rowd[0:1, :], L1)
    ndr = inc(b, d, "d_row", 16)
    wt(b, V, "d_row", ndr)
    V.tensor_tensor(out=b.rwr[0:1, :], in0=L1, in1=L1, op=OP.mult)
    V.tensor_tensor(out=b.rwr[0:1, :], in0=L2, in1=b.rwr[0:1, :], op=OP.subtract)
    nvv = dinc(b, V, "vev", 1)
    wt(b, A, "vev", nvv)
    A.activation(out=b.rwr[0:1, :], in_=b.rwr[0:1, :], func=AF.Sqrt,
                 bias=b.cst[0:1, 0:1], scale=1.0)
    nav = dinc(b, A, "aev", 1)
    wt(b, V, "aev", nav)
    V.reciprocal(out=b.rwr[0:1, :], in_=b.rwr[0:1, :])
    nvv = dinc(b, V, "vev", 1)
    wt(b, Y, "vev", nvv)
    d = Y.dma_start(b.rowd[1:2, :], b.rwr[0:1, :])
    ndr = inc(b, d, "d_row", 16)
    wt(b, Y, "d_row", ndr)
    d = Y.dma_start(home(b, 0, 2048), b.rowd[0:1, :].to_broadcast((128, 2048)))
    inc(b, d, "d_bc", 16)
    d = Y.dma_start(home(b, 2048, 2048), b.rowd[1:2, :].to_broadcast((128, 2048)))
    nbc = inc(b, d, "d_bc", 16)
    wt(b, V, "d_bc", nbc)
    wt(b, V, "aux4", cnt(b, "aux4"))

    for c in range(NCH):
        if stream:
            xcb = b.xf[:, (c % 2) * 2048:(c % 2) * 2048 + 2048]
            x2sem = "d_x2a" if (c % 2) == 0 else "d_x2b"
            wt(b, Y, "vev", vdone[NCH + c - 2] if c >= 2 else vdone[NCH - 2 + c])
            d = Y.dma_start(xcb, b.spillB[c])
            nld = inc(b, d, x2sem, 16)
            wt(b, V, x2sem, nld)
        else:
            xcb = xslot(b, c)
            if c >= 1:
                wt(b, Y, "vev", vdone[-1])
            d = Y.dma_start(spill[c], xcb)
            nsp = inc(b, d, "d_sp", 16)
        V.tensor_tensor(out=lnsc(b), in0=xcb, in1=home(b, 0, 2048), op=OP.subtract)
        V.tensor_tensor(out=lnsc(b), in0=lnsc(b), in1=home(b, 2048, 2048), op=OP.mult)
        if not stream:
            wt(b, V, "d_sp", nsp)
        dstap = (hslot(b, c) if dst_xb else b.qb[:, c * 2048:(c + 1) * 2048])
        V.tensor_scalar(out=dstap, in0=lnsc(b),
                        scalar1=b.modv[:, ca_col + c:ca_col + c + 1],
                        scalar2=b.modv[:, sa_col + c:sa_col + c + 1],
                        op0=OP.mult, op1=OP.add)
        nvv = dinc(b, V, "vev", 1)
        vdone.append(nvv)


def p1_ln1(b, bi):
    barrier(b)
    ln_phase(b, bi, src_resident=True, bchome="qb", ca_col=8, sa_col=0, dst_xb=True,
             spill=b.spillA)
    if b.debug:
        nc = b.nc
        V, Y = nc.vector, nc.sync
        for c in range(NCH):
            wt(b, V, "vev")
            V.tensor_copy(lnsc(b), hslot(b, c))
            n = dinc(b, V, "vev", 1)
            wt(b, Y, "vev", n)
            dump(b, lnsc(b), b.dbg_h[c])
            wt(b, V, "d_out")


def p4_ln2(b, bi):
    barrier(b)
    ln_phase(b, bi, src_resident=False, bchome="kb", ca_col=32, sa_col=24, dst_xb=False,
             spill=None)
    if b.debug:
        nc = b.nc
        V, Y = nc.vector, nc.sync
        for c in range(NCH):
            wt(b, V, "vev")
            V.tensor_copy(lnsc(b), b.qb[:, c * 2048:(c + 1) * 2048])
            n = dinc(b, V, "vev", 1)
            wt(b, Y, "vev", n)
            dump(b, lnsc(b), b.dbg_h2[c])
            wt(b, V, "d_out")


def p2_qkv(b, bi):
    nc = b.nc
    T, V, A, G, Y = nc.tensor, nc.vector, nc.scalar, nc.gpsimd, nc.sync
    barrier(b)
    wb = b.sr[:, 0:8192]
    h = lambda c: hslot(b, c)

    wt(b, A, "d_bq", 32)
    for wi, (wdr, dst, bcol) in enumerate([(b.wq, b.qb, 0), (b.wk, b.kb, 8)]):
        if wi > 0:
            wt(b, G, "tmm", cnt(b, "tmm"))
        _d, _s = ld3(wb, wdr, 1024)
        d = G.dma_start(_d, _s)
        ndw = inc(b, d, "d_w", 16)
        wt(b, T, "d_w", ndw)
        for i in range(NCH * NBLK):
            oc, blk = divmod(i, NBLK)
            pss = b.psA if (i % 2) == 0 else b.psB
            bank = (i // 2) % 4
            ps = pss[:, bank * 512:(bank + 1) * 512]
            if i >= 8:
                wt(b, T, "aev", cnt(b, "aev") - 7)
            for kc in range(NCH):
                mm = T.matmul(ps, wb[:, kc * 1024 + oc * 128:kc * 1024 + oc * 128 + 128],
                              h(kc)[:, blk * 512:(blk + 1) * 512],
                              start=(kc == 0), stop=(kc == NCH - 1))
            ntm = inc(b, mm, "tmm", 1)
            wt(b, A, "tmm", ntm)
            ev = A.activation(out=dst[:, oc * 2048 + blk * 512:oc * 2048 + blk * 512 + 512],
                              in_=ps, func=AF.Identity, bias=b.bqs[:, bcol + oc:bcol + oc + 1])
            inc(b, ev, "aev", 1)

    wt(b, G, "tmm", cnt(b, "tmm"))
    _d, _s = ld3(wb, b.wv, 1024)
    d = G.dma_start(_d, _s)
    ndw = inc(b, d, "d_w", 16)
    wt(b, V, "aev", cnt(b, "aev"))
    vv = b.vb[:].rearrange("p (t hh j) -> p t hh j", hh=16, j=65)
    V.memset(vv[:, :, :, 64:65], 1.0)
    dinc(b, V, "vev", 1)
    wt(b, T, "d_w", ndw)
    wt(b, A, "vev", cnt(b, "vev"))
    for i in range(NT * 2):
        tt, half = divmod(i, 2)
        pss = b.psA if (i % 2) == 0 else b.psB
        bank = (i // 2) % 4
        ps = pss[:, bank * 512:(bank + 1) * 512]
        if i >= 8:
            wt(b, T, "aev", cnt(b, "aev") - 7)
        for kc in range(NCH):
            mm = T.matmul(ps, h(kc)[:, tt * 128:(tt + 1) * 128],
                          wb[:, kc * 1024 + half * 512:kc * 1024 + half * 512 + 512],
                          start=(kc == 0), stop=False)
        mm = T.matmul(ps, b.onesr[0:1, 0:128],
                      b.bvrow[0:1, half * 512:(half + 1) * 512], start=False, stop=True)
        ntm = inc(b, mm, "tmm", 1)
        wt(b, A, "tmm", ntm)
        dstv = (b.vb[:, tt * 1040 + half * 520:tt * 1040 + half * 520 + 520]
                .rearrange("p (hh j) -> p hh j", j=65)[:, :, 0:64])
        ev = A.activation(out=dstv, in_=ps.rearrange("p (hh j) -> p hh j", j=64), func=AF.Copy)
        inc(b, ev, "aev", 1)

    if b.debug:
        wt(b, V, "aev")
        V.tensor_copy(otsv(b, 2048), b.qb[:, 0:2048])
        n = dinc(b, V, "vev", 1)
        wt(b, Y, "vev", n); dump(b, otsv(b, 2048), b.dbg_q); wt(b, V, "d_out")
        V.tensor_copy(otsv(b, 2048), b.kb[:, 0:2048])
        n = dinc(b, V, "vev", 1)
        wt(b, Y, "vev", n); dump(b, otsv(b, 2048), b.dbg_k); wt(b, V, "d_out")
        V.tensor_copy(otsv(b, 2048), b.vb[:, 0:2048])
        n = dinc(b, V, "vev", 1)
        wt(b, Y, "vev", n); dump(b, otsv(b, 2048), b.dbg_v[:, 0:2048]); wt(b, V, "d_out")


def p3_attention(b, bi):
    nc = b.nc
    T, V, A, G, Y = nc.tensor, nc.vector, nc.scalar, nc.gpsimd, nc.sync
    barrier(b)

    wt(b, T, "aux2", 1)
    for hh in range(NH):
        for c in range(NT):
            mm = T.matmul(b.psA[0:65, hh:hh + 1],
                          b.vb[:, c * 1040 + hh * 65:c * 1040 + hh * 65 + 65],
                          b.onesc[:], start=(c == 0), stop=(c == NT - 1),
                          skip_group_check=True)
    ntm = inc(b, mm, "tmm", 1)
    wt(b, A, "tmm", ntm)
    ev = A.activation(out=b.vsums[:], in_=b.psA[0:65, 0:16], func=AF.Copy)
    nav = inc(b, ev, "aev", 1)
    wt(b, T, "aev", nav)

    import os as _os
    for pair in range(int(_os.environ.get("KPAIRS", str(NPAIR)))):
        attn_pair(b, bi, pair)

    wb = b.vb[:, 0:8192]
    wt(b, G, "sfree", cnt(b, "sfree"))
    _d, _s = ld3(wb, b.wo, 1024)
    d = G.dma_start(_d, _s)
    ndw = inc(b, d, "d_w", 16)
    wt(b, T, "d_w", ndw)
    on = lambda c: onorm(b, c)
    wt(b, Y, "sfree", cnt(b, "sfree"))
    for oc in range(NCH):
        xcb = srf(b, (oc % 2) * 2048, 2048)
        x2sem = "d_x2a" if (oc % 2) == 0 else "d_x2b"
        if oc >= 2:
            wt(b, Y, "resid", cnt(b, "resid") - NBLK)
            wt(b, Y, "d_sp", cnt(b, "d_sp"))
        d = Y.dma_start(xcb, b.spillA[oc])
        nld = inc(b, d, x2sem, 16)
        for blk in range(NBLK):
            i = oc * NBLK + blk
            pss = b.psA if (i % 2) == 0 else b.psB
            bank = (i // 2) % 4
            ps = pss[:, bank * 512:(bank + 1) * 512]
            if i >= 8:
                wt(b, T, "resid", cnt(b, "resid") - 7)
            for kc in range(NCH):
                mm = T.matmul(ps, wb[:, kc * 1024 + oc * 128:kc * 1024 + oc * 128 + 128],
                              on(kc)[:, blk * 512:(blk + 1) * 512],
                              start=(kc == 0), stop=False)
            mm = T.matmul(ps, b.borow[0:1, oc * 128:(oc + 1) * 128],
                          b.onesr[0:1, 0:512], start=False, stop=True)
            ntm = inc(b, mm, "tmm", 1)
            wt(b, V, "tmm", ntm)
            wt(b, V, x2sem, nld)
            st = V.scalar_tensor_tensor(out=xcb[:, blk * 512:(blk + 1) * 512],
                                        in0=ps, scalar=b.modv[:, 16 + oc:16 + oc + 1],
                                        in1=xcb[:, blk * 512:(blk + 1) * 512],
                                        op0=OP.mult, op1=OP.add)
            inc(b, st, "resid", 1)
        wt(b, Y, "resid", cnt(b, "resid"))
        d = Y.dma_start(b.spillB[oc], xcb)
        inc(b, d, "d_sp", 16)
    wt(b, Y, "d_sp")


def attn_pair(b, bi, pair):
    nc = b.nc
    T, V, A, G, Y = nc.tensor, nc.vector, nc.scalar, nc.gpsimd, nc.sync
    hA, hB = 2 * pair, 2 * pair + 1

    def qsl(h, c0, n):
        p0 = (h % 2) * 64
        return b.qb[p0:p0 + 64, (h // 2) * 2048 + c0:(h // 2) * 2048 + c0 + n]

    def ksl(h, c0, n):
        p0 = (h % 2) * 64
        return b.kb[p0:p0 + 64, (h // 2) * 2048 + c0:(h // 2) * 2048 + c0 + n]

    def s16ap(u):
        s = (b.guctr + u) % SRING
        if s < 5:
            return b.sr[:, s * 2048:(s + 1) * 2048]
        return xgsl(b, SRING_XG[s - 5] * 2048, 2048)

    wt(b, V, "ethx", cnt(b, "ethx"))
    V.memset(b.st_lo[:], NEG)
    V.memset(b.st_hi[:], 1.0e30)
    V.memset(b.st_chi[:], 0.0)
    V.memset(b.st_clo[:], float(S))
    p = V.memset(b.st_pr[:], T0)
    vg(b, p)
    dinc(b, V, "upd", 1)

    wt(b, T, "p2e", cnt(b, "p2e"))
    evposts = {}
    def emit_probes(units, r, upd_need):
        dve = r in DVE_ROUNDS
        for u in units:
            if dve:
                wt(b, V, "aev", evposts[u])
                wt(b, V, "upd", upd_need)
                pr = V.tensor_scalar(out=xgsl(b, XG_IND, 2048),
                                     in0=s16ap(u), scalar1=b.st_pr[:, u:u + 1],
                                     scalar2=0.0, op0=OP.is_ge, op1=OP.add,
                                     accum_out=b.st_cn[:, u:u + 1])
                inc(b, pr, "prd", 1)
            else:
                wt(b, A, "aev", evposts[u])
                wt(b, A, "upd", upd_need)
                pr = A.activation(out=xgsl(b, XG_E, 2048),
                                  in_=s16ap(u), func=AF.Sign,
                                  bias=b.st_pr[:, u:u + 1], scale=-1.0,
                                  accum_out=b.st_cn[:, u:u + 1])
                inc(b, pr, "pra", 1)
        return cnt(b, "prd" if dve else "pra")

    def emit_update(units, r, npr):
        dve = r in DVE_ROUNDS
        step_r = STEP0 * (2.0 ** r)
        u0, n = units[0], len(units)
        L = lambda t: t[:, u0:u0 + n]
        if dve:
            wt(b, V, "prd", npr)
        else:
            wt(b, V, "pra", npr)
            p = V.tensor_scalar(out=L(b.st_cn), in0=L(b.st_cn), scalar1=-0.5,
                                scalar2=1024.0, op0=OP.mult, op1=OP.add)
            vg(b, p)
        p = V.tensor_scalar(out=L(b.st_s1), in0=L(b.st_cn), scalar1=float(KEEP),
                            scalar2=None, op0=OP.is_ge)
        vg(b, p)
        m1 = L(b.st_s1).bitcast(I32)
        V.copy_predicated(out=L(b.st_lo), mask=m1, data=L(b.st_pr))
        V.copy_predicated(out=L(b.st_clo), mask=m1, data=L(b.st_cn))
        p = V.tensor_scalar(out=L(b.st_s2), in0=L(b.st_s1), scalar1=-1.0, scalar2=1.0,
                            op0=OP.mult, op1=OP.add)
        vg(b, p)
        m2 = L(b.st_s2).bitcast(I32)
        V.copy_predicated(out=L(b.st_hi), mask=m2, data=L(b.st_pr))
        p = V.copy_predicated(out=L(b.st_chi), mask=m2, data=L(b.st_cn))
        if r < RPROBE - 1:
            vg(b, p)
            V.tensor_scalar(out=L(b.st_s1), in0=L(b.st_lo), scalar1=-1.0e29, scalar2=None,
                            op0=OP.is_le)
            V.tensor_scalar(out=L(b.st_s2), in0=L(b.st_hi), scalar1=1.0e29, scalar2=None,
                            op0=OP.is_ge)
            p = V.tensor_tensor(out=L(b.st_u1), in0=L(b.st_hi), in1=L(b.st_lo),
                                op=OP.subtract)
            p = V.tensor_tensor(out=L(b.st_u2), in0=L(b.st_clo), in1=L(b.st_chi),
                                op=OP.subtract)
            vg(b, p)
            p = V.reciprocal(out=L(b.st_u2), in_=L(b.st_u2))
            p2 = V.tensor_scalar(out=L(b.st_cn), in0=L(b.st_cn), scalar1=-float(KEEP),
                                 scalar2=None, op0=OP.add)
            vg(b, p2)
            p = V.tensor_tensor(out=L(b.st_u2), in0=L(b.st_u2), in1=L(b.st_cn), op=OP.mult)
            vg(b, p)
            p = V.tensor_tensor(out=L(b.st_u2), in0=L(b.st_u2), in1=L(b.st_u1), op=OP.mult)
            vg(b, p)
            p = V.tensor_tensor(out=L(b.st_u2), in0=L(b.st_pr), in1=L(b.st_u2), op=OP.add)
            p2 = V.tensor_scalar(out=L(b.st_u1), in0=L(b.st_u1), scalar1=0.05, scalar2=None,
                                 op0=OP.mult)
            vg(b, p2)
            p = V.tensor_tensor(out=L(b.st_s3), in0=L(b.st_lo), in1=L(b.st_u1), op=OP.add)
            vg(b, p)
            p = V.tensor_tensor(out=L(b.st_u2), in0=L(b.st_u2), in1=L(b.st_s3), op=OP.max)
            p2 = V.tensor_tensor(out=L(b.st_s3), in0=L(b.st_hi), in1=L(b.st_u1),
                                 op=OP.subtract)
            vg(b, p2)
            p = V.tensor_tensor(out=L(b.st_pr), in0=L(b.st_u2), in1=L(b.st_s3), op=OP.min)
            p2 = V.tensor_scalar(out=L(b.st_u1), in0=L(b.st_sd),
                                 scalar1=float(SSTEP * (2.0 ** r)), scalar2=None, op0=OP.mult)
            vg(b, p2)
            p = V.tensor_tensor(out=L(b.st_s3), in0=L(b.st_hi), in1=L(b.st_u1),
                                op=OP.subtract)
            vg(b, p)
            V.copy_predicated(out=L(b.st_pr), mask=L(b.st_s1).bitcast(I32), data=L(b.st_s3))
            p = V.tensor_tensor(out=L(b.st_s3), in0=L(b.st_lo), in1=L(b.st_u1), op=OP.add)
            vg(b, p)
            p = V.copy_predicated(out=L(b.st_pr), mask=L(b.st_s2).bitcast(I32),
                                  data=L(b.st_s3))
            vg(b, p)
        else:
            vg(b, p)
        dinc(b, V, "upd", 1)
        return cnt(b, "upd")

    def emit_fixup_masks(units8):
        ps = []
        for u in units8:
            ps.append(V.scalar_tensor_tensor(out=s16ap(u), in0=s16ap(u),
                                             scalar=b.st_hi[:, u:u + 1], in1=s16ap(u),
                                             op0=OP.is_lt, op1=OP.mult))
        vg(b, ps[-1])
        for u in units8:
            mx = V.max(out=b.w9[:, u * 9:u * 9 + 8], in_=s16ap(u))
            b.sfree_post[b.guctr + u] = inc(b, mx, "sfree", 1)
            V.tensor_copy(b.w9[:, u * 9 + 8:u * 9 + 9], b.st_lo[:, u:u + 1])

    def emit_fixup_picks(units8):
        u0s = units8[0]
        Lp = lambda t: t[:, u0s:u0s + 2 * GW]
        scr = lambda k: b.w9[:, 288 + k * 9:288 + k * 9 + 9]
        p = V.tensor_scalar(out=Lp(b.st_s1), in0=Lp(b.st_chi), scalar1=-1.0, scalar2=203.0,
                            op0=OP.mult, op1=OP.add)
        vg(b, p)
        V.tensor_scalar(out=Lp(b.st_u2), in0=Lp(b.st_s1), scalar1=1.0, scalar2=None,
                        op0=OP.add)
        for i, u in enumerate(units8):
            p = V.tensor_scalar(out=scr(2 * i), in0=b.iota9[:],
                                scalar1=b.st_s1[:, u:u + 1], scalar2=None, op0=OP.is_equal)
        vg(b, p)
        for i, u in enumerate(units8):
            V.tensor_scalar(out=scr(2 * i + 1), in0=b.iota9[:],
                            scalar1=b.st_u2[:, u:u + 1], scalar2=None, op0=OP.is_equal)
            p = V.tensor_tensor(out=scr(2 * i), in0=b.w9[:, u * 9:u * 9 + 9],
                                in1=scr(2 * i), op=OP.mult)
        vg(b, p)
        for i, u in enumerate(units8):
            V.tensor_tensor(out=scr(2 * i + 1), in0=b.w9[:, u * 9:u * 9 + 9],
                            in1=scr(2 * i + 1), op=OP.mult)
            p = V.tensor_scalar(out=scr(2 * i), in0=scr(2 * i), scalar1=0.0, scalar2=0.0,
                                op0=OP.add, op1=OP.add, accum_out=b.st_s2[:, u:u + 1])
        vg(b, p)
        for i, u in enumerate(units8):
            p = V.tensor_scalar(out=scr(2 * i + 1), in0=scr(2 * i + 1), scalar1=0.0,
                                scalar2=0.0, op0=OP.add, op1=OP.add,
                                accum_out=b.st_s3[:, u:u + 1])
        vg(b, p)
        V.tensor_tensor(out=Lp(b.st_s2), in0=Lp(b.st_s2), in1=Lp(b.st_s3), op=OP.add)
        V.tensor_scalar(out=Lp(b.st_u1), in0=Lp(b.st_s1), scalar1=6.5, scalar2=None,
                        op0=OP.is_le)
        V.tensor_scalar(out=Lp(b.st_sm), in0=Lp(b.st_s1), scalar1=6.5, scalar2=None,
                        op0=OP.is_gt)
        p = V.tensor_scalar(out=Lp(b.st_sq), in0=Lp(b.st_clo), scalar1=204.0, scalar2=None,
                            op0=OP.is_equal)
        vg(b, p)
        V.tensor_scalar(out=Lp(b.st_s2), in0=Lp(b.st_s2), scalar1=0.5, scalar2=None,
                        op0=OP.mult)
        p = V.tensor_tensor(out=Lp(b.st_sm), in0=Lp(b.st_sm), in1=Lp(b.st_sq), op=OP.mult)
        vg(b, p)
        V.tensor_tensor(out=Lp(b.st_u1), in0=Lp(b.st_u1), in1=Lp(b.st_sm), op=OP.add)
        p = V.tensor_scalar(out=Lp(b.st_sd), in0=Lp(b.st_s1), scalar1=-0.5, scalar2=None,
                            op0=OP.is_ge)
        vg(b, p)
        V.tensor_tensor(out=Lp(b.st_u1), in0=Lp(b.st_u1), in1=Lp(b.st_sd), op=OP.mult)
        p = V.tensor_copy(Lp(b.st_ts), Lp(b.st_lo))
        vg(b, p)
        p = V.copy_predicated(out=Lp(b.st_ts), mask=Lp(b.st_u1).bitcast(I32),
                              data=Lp(b.st_s2))
        vg(b, p)

    upd0 = cnt(b, "upd")
    for gp in range(32 // (2 * GW)):
        for j in range(gp * GW, (gp + 1) * GW):
            wt(b, T, "aev", cnt(b, "aev"))
            for blk in range(NBLK):
                mm = T.matmul(b.psA[:, blk * 512:(blk + 1) * 512],
                              qsl(hA, j * 128, 128), ksl(hA, blk * 512, 512),
                              start=True, stop=True, tile_position=(0, 0))
                mm = T.matmul(b.psB[:, blk * 512:(blk + 1) * 512],
                              qsl(hB, j * 128, 128), ksl(hB, blk * 512, 512),
                              start=True, stop=True, tile_position=(64, 0))
            ntm = inc(b, mm, "tmm", 1)
            wt(b, A, "tmm", ntm)
            wt(b, V, "tmm", ntm)
            u0_, u1_ = 2 * j, 2 * j + 1

            def vevac(u, ps):
                prev = b.sfree_post.get(b.guctr + u - SRING)
                if prev is not None:
                    wt(b, V, "sfree", prev)
                for blk in range(NBLK):
                    ev = V.tensor_scalar(out=s16ap(u)[:, blk * 512:(blk + 1) * 512],
                                         in0=ps[:, blk * 512:(blk + 1) * 512],
                                         scalar1=0.125, scalar2=None, op0=OP.mult)
                return inc(b, ev, "aev", 1)

            evposts[u1_] = vevac(u1_, b.psB)
            sqp = A.activation(out=xgsl(b, XG_E, 2048), in_=b.psA[:], func=AF.Square,
                               scale=0.125, accum_out=b.st_sq[:, u0_:u0_ + 1])
            nsq0 = inc(b, sqp, "aev", 1)
            wt(b, A, "aev", evposts[u1_])
            sqp = A.activation(out=xgsl(b, XG_E, 2048), in_=b.psB[:], func=AF.Square,
                               scale=0.125, accum_out=b.st_sq[:, u1_:u1_ + 1])
            inc(b, sqp, "aev", 1)
            wt(b, V, "aev", nsq0)
            evposts[u0_] = vevac(u0_, b.psA)
        u0s = 2 * gp * GW
        Ls = lambda t: t[:, u0s:u0s + 2 * GW]
        wt(b, V, "aev", cnt(b, "aev"))
        p = V.tensor_scalar(out=Ls(b.st_sq), in0=Ls(b.st_sq), scalar1=1.0 / S, scalar2=None,
                            op0=OP.mult)
        vg(b, p)
        nvv = dinc(b, V, "vev", 1)
        wt(b, A, "vev", nvv)
        sq = A.activation(out=Ls(b.st_sd), in_=Ls(b.st_sq), func=AF.Sqrt)
        nav = inc(b, sq, "aev", 1)
        wt(b, V, "aev", nav)
        p = V.tensor_scalar(out=Ls(b.st_pr), in0=Ls(b.st_sd), scalar1=ZINIT, scalar2=None,
                            op0=OP.mult)
        vg(b, p)
        upd0gp = dinc(b, V, "upd", 1)
        halves = [list(range((2 * gp + h) * GW, (2 * gp + h + 1) * GW)) for h in (0, 1)]
        upd_post = [upd0gp, upd0gp]
        for r in range(RPROBE):
            npr0 = emit_probes(halves[0], r, upd_post[0])
            npr1 = emit_probes(halves[1], r, upd_post[1])
            upd_post[0] = emit_update(halves[0], r, npr0)
            upd_post[1] = emit_update(halves[1], r, npr1)
        emit_fixup_masks(halves[0] + halves[1])
        emit_fixup_picks(halves[0] + halves[1])

    b.guctr += 32
    npick = dinc(b, V, "ethx", 1)
    if b.debug:
        wt(b, Y, "ethx", npick)
        dump(b, b.st_ts[:], b.dbg_ts[bi, pair])

    import os as _os
    if _os.environ.get("KNOP2", "0") == "1":
        return
    wt(b, A, "ethx", npick)
    A.activation(out=b.eth[:].rearrange("p (two j) -> p two j", j=16),
                 in_=b.st_ts[:].rearrange("p (j two) -> p two j", two=2), func=AF.Exp)
    nav = dinc(b, A, "ethx", 1)
    wt(b, T, "ethx", cnt(b, "ethx"))
    wt(b, T, "aev", cnt(b, "aev"))
    tr = T.transpose(b.psB[0:32, 0:128], b.eth[:], b.identf[:])
    ntm = inc(b, tr, "tmm", 1)
    wt(b, A, "tmm", ntm)
    ev = A.activation(out=b.ethT[:], in_=b.psB[0:32, 0:128], func=AF.Copy)
    nav = inc(b, ev, "aev", 1)
    wt(b, Y, "aev", nav)
    for hi_ in range(2):
        d = Y.dma_start(b.ethd[hi_:hi_ + 1, :].rearrange("a (t q) -> (a t) q", q=128),
                        b.ethT[hi_ * 16:(hi_ + 1) * 16, :])
        inc(b, d, "d_eth", 16)
    wt(b, G, "d_eth", cnt(b, "d_eth"))
    for hi_ in range(2):
        d = G.dma_start(xgsl(b, XG_ETH + hi_ * 2048, 2048),
                        b.ethd[hi_:hi_ + 1, :].to_broadcast((128, 2048)))
        inc(b, d, "d_bce", 16)
    nbc = cnt(b, "d_bce")
    if _os.environ.get("KNOPV", "0") == "1":
        return

    wt(b, T, "aev", cnt(b, "aev"))
    for hi_, h in ((0, hA), (1, hB)):
        ethbc = xgsl(b, XG_ETH + hi_ * 2048, 2048)
        for c in range(NT):
            es = c % 2
            eap = xgsl(b, XG_E + es * 2048, 2048)
            emap = xgsl(b, XG_EM, 2048)
            gap = xgsl(b, XG_G + es * 2048, 2048)
            indap = xgsl(b, XG_IND, 2048)
            if c > 0 or hi_ > 0:
                wt(b, T, "p2e", cnt(b, "p2e"))
            for blk in range(NBLK):
                mm = T.matmul(b.psA[:, blk * 512:(blk + 1) * 512],
                              ksl(h, c * 128, 128), qsl(h, blk * 512, 512),
                              start=True, stop=True,
                              tile_position=((h % 2) * 64, 0))
            ntm = inc(b, mm, "tmm", 1)
            wt(b, A, "tmm", ntm)
            if c >= 2:
                wt(b, A, "p2g", cnt(b, "p2g") - 1)
            ev = A.activation(out=eap, in_=b.psA[:], func=AF.Exp, scale=0.125)
            nev = inc(b, ev, "p2e", 1)
            wt(b, V, "p2e", nev)
            if hi_ == 0 and c == 0:
                wt(b, V, "d_bce", nbc)
            V.tensor_scalar(out=emap, in0=eap, scalar1=-1.0, scalar2=None, op0=OP.add)
            p = V.tensor_tensor(out=indap, in0=eap, in1=ethbc, op=OP.is_ge)
            vg(b, p)
            g16 = V.tensor_tensor(out=gap, in0=emap, in1=indap, op=OP.mult)
            ngg = inc(b, g16, "p2g", 1)
            wt(b, T, "p2g", ngg)
            for blk in range(NBLK):
                mm = T.matmul(b.psB[0:65, blk * 512:(blk + 1) * 512],
                              b.vb[:, c * 1040 + h * 65:c * 1040 + h * 65 + 65],
                              gap[:, blk * 512:(blk + 1) * 512],
                              start=(c == 0), stop=(c == NT - 1))
            inc(b, mm, "tmm", 1)
        if _os.environ.get("KNOOC", "0") == "1":
            continue
        npv = cnt(b, "tmm")
        wt(b, V, "tmm", npv)
        wt(b, T, "p2e", cnt(b, "p2e"))
        for blk in range(NBLK):
            if blk > 0:
                wt(b, V, "tmm", cnt(b, "tmm"))
            zst = b.o1[0:1, 0:512]
            p = V.tensor_scalar(out=zst, in0=b.psB[64:65, blk * 512:(blk + 1) * 512],
                                scalar1=float(S), scalar2=None, op0=OP.add)
            vg(b, p)
            p = V.reciprocal(out=zst, in_=zst)
            vg(b, p)
            p = V.tensor_copy(b.rzb[0:1, 0:512], zst)
            vg(b, p)
            nvv = dinc(b, V, "vev", 1)
            wt(b, T, "vev", nvv)
            mm = T.matmul(b.psA[0:64, blk * 512:(blk + 1) * 512], b.onesr[0:1, 0:64],
                          b.rzb[0:1, 0:512], start=True, stop=True)
            ntm = inc(b, mm, "tmm", 1)
            wt(b, V, "tmm", ntm)
            p = V.tensor_scalar(out=b.o1[:], in0=b.psB[0:64, blk * 512:(blk + 1) * 512],
                                scalar1=b.vsums[0:64, h:h + 1], scalar2=None, op0=OP.add)
            vg(b, p)
            p0 = (h % 2) * 64
            onap = (b.xf[p0:p0 + 64,
                         (h // 2) * 1024 + blk * 256:(h // 2) * 1024 + blk * 256 + 256]
                    .bitcast(BF16))
            v = V.tensor_tensor(out=onap, in0=b.o1[:],
                                in1=b.psA[0:64, blk * 512:(blk + 1) * 512], op=OP.mult)
            inc(b, v, "p2e", 1)


def p5_router(b, bi):
    nc = b.nc
    T, V, A, G, Y = nc.tensor, nc.vector, nc.scalar, nc.gpsimd, nc.sync
    barrier(b)
    h2 = lambda c: b.qb[:, c * 2048:(c + 1) * 2048]
    aevposts = []
    for tt in range(NT):
        bank = tt % 4
        ps = b.psA[:, bank * 512:bank * 512 + 8]
        if tt >= 4:
            wt(b, T, "aev", aevposts[tt - 4])
        for kc in range(NCH):
            mm = T.matmul(ps, h2(kc)[:, tt * 128:(tt + 1) * 128], b.rwtb[:, kc * 8:(kc + 1) * 8],
                          start=(kc == 0), stop=False)
        mm = T.matmul(ps, b.onesr[0:1, 0:128], b.rbrow[0:1, :], start=False, stop=True)
        ntm = inc(b, mm, "tmm", 1)
        wt(b, V, "tmm", ntm)
        p = V.tensor_reduce(out=b.prsc[:, 0:1], in_=ps, axis=mybir.AxisListType.X, op=OP.max)
        vg(b, p)
        p = V.tensor_scalar(out=b.prsc[:, 1:2], in0=b.prsc[:, 0:1], scalar1=-1.0,
                            scalar2=None, op0=OP.mult)
        vg(b, p)
        nvv = dinc(b, V, "vev", 1)
        wt(b, A, "vev", nvv)
        ev = A.activation(out=b.prsc[:, 4:12], in_=ps, func=AF.Exp,
                          bias=b.prsc[:, 1:2], scale=1.0, accum_out=b.prsc[:, 2:3])
        nav = inc(b, ev, "aev", 1)
        aevposts.append(nav)
        wt(b, V, "aev", nav)
        p = V.reciprocal(out=b.prsc[:, 3:4], in_=b.prsc[:, 2:3])
        vg(b, p)
        p = V.tensor_scalar(out=b.prsc[:, 4:12], in0=b.prsc[:, 4:12], scalar1=b.prsc[:, 3:4],
                            scalar2=None, op0=OP.mult)
        vg(b, p)
        p = V.max(out=b.prsc[:, 12:20], in_=b.prsc[:, 4:12])
        vg(b, p)
        st = V.scalar_tensor_tensor(out=b.wall[:, tt * 8:(tt + 1) * 8], in0=b.prsc[:, 4:12],
                                    scalar=b.prsc[:, 13:14], in1=b.prsc[:, 4:12],
                                    op0=OP.is_ge, op1=OP.mult)
        inc(b, st, "vev", 2)
    if b.debug:
        wt(b, Y, "vev")
        dump(b, b.wall[:], b.dbg_w)


def p6_ffn(b, bi):
    nc = b.nc
    T, V, A, G, Y = nc.tensor, nc.vector, nc.scalar, nc.gpsimd, nc.sync
    barrier(b)
    h2 = lambda c: b.qb[:, c * 2048:(c + 1) * 2048]
    w1b = b.kb[:, 0:16384]
    w2b = b.vb[:, 0:16384]
    h1b = b.sr[:, 0:8192]
    mf = lambda tt: mixf(b, tt)

    for e in range(-1, NE):
        if e == -1:
            a1, b1s, a2, b2s = b.sw1, b.sb1.rearrange("(c p) -> p c", p=128), \
                b.sw2, b.sb2[None, :]
        else:
            a1, b1s, a2, b2s = b.rw1[e], b.rb1[e].rearrange("(c p) -> p c", p=128), \
                b.rw2[e], b.rb2[e][None, :]
        wt(b, G, "ff1", cnt(b, "ff1"))
        wt(b, G, "ffg", cnt(b, "ffg"))
        _d, _s = ld3(w1b, a1, FH)
        d = G.dma_start(_d, _s)
        inc(b, d, "d_w1", 16)
        d = G.dma_start(b.b1ch[:, 0:16], b1s, allow_slow_non_contiguous=True)
        ndw1 = inc(b, d, "d_w1", 16)
        wt(b, G, "ff2", cnt(b, "ff2"))
        wt(b, G, "ffm", cnt(b, "ffm"))
        _d, _s = ld3(w2b, a2, D)
        d = G.dma_start(_d, _s)
        inc(b, d, "d_w2", 16)
        d = G.dma_start(b.b2row[0:1, :], b2s)
        ndw = inc(b, d, "d_w2", 16)
        for tb in range(4):
            for oc in range(16):
                i = oc % 2
                ps = (b.psA if i == 0 else b.psB)[:, 0:512]
                if oc >= 2:
                    wt(b, T, "ffg", cnt(b, "ffg") - 1)
                wt(b, T, "d_w1", ndw1)
                last1 = None
                for kc in range(NCH):
                    mm = T.matmul(ps, w1b[:, kc * 2048 + oc * 128:kc * 2048 + oc * 128 + 128],
                                  h2(kc)[:, tb * 512:(tb + 1) * 512],
                                  start=(kc == 0), stop=(kc == NCH - 1))
                ntm = inc(b, mm, "tmm", 1)
                if tb == 3 and oc == 15:
                    dinc(b, T, "ff1", 1)
                wt(b, A, "tmm", ntm)
                ev = A.activation(out=h1b[:, oc * 512:(oc + 1) * 512], in_=ps, func=AF.Gelu,
                                  bias=b.b1ch[:, oc:oc + 1])
                inc(b, ev, "ffg", 1)
            ng = cnt(b, "ffg")
            wt(b, T, "ffg", ng)
            wt(b, T, "d_w2", ndw)
            for ts_ in range(4):
                tt = tb * 4 + ts_
                wcol = b.wall[:, tt * 8 + e:tt * 8 + e + 1] if e >= 0 else None
                for dh in range(2):
                    i = (ts_ * 2 + dh) % 2
                    ps = (b.psA if i == 0 else b.psB)[:, 512:1024]
                    if ts_ * 2 + dh >= 2:
                        wt(b, T, "ffm", cnt(b, "ffm") - 1)
                    for oc in range(16):
                        mm = T.matmul(ps, h1b[:, oc * 512 + ts_ * 128:oc * 512 + ts_ * 128 + 128],
                                      w2b[:, oc * 1024 + dh * 512:oc * 1024 + dh * 512 + 512],
                                      start=(oc == 0), stop=False)
                    mm = T.matmul(ps, b.onesr[0:1, 0:128],
                                  b.b2row[0:1, dh * 512:(dh + 1) * 512], start=False, stop=True)
                    ntm = inc(b, mm, "tmm", 1)
                    if tb == 3 and ts_ == 3 and dh == 1:
                        dinc(b, T, "ff2", 1)
                    wt(b, V, "tmm", ntm)
                    dst = mf(tt)[:, dh * 512:(dh + 1) * 512]
                    if e == -1:
                        v = V.tensor_scalar(out=dst, in0=ps, scalar1=1.0, scalar2=None,
                                            op0=OP.mult)
                    else:
                        v = V.scalar_tensor_tensor(out=dst, in0=ps, scalar=wcol, in1=dst,
                                                   op0=OP.mult, op1=OP.add)
                    inc(b, v, "ffm", 1)
        if b.debug and e in (-1, NE - 1):
            dbg = b.dbg_mx0 if e == -1 else b.dbg_mx1
            wt(b, Y, "ffm", cnt(b, "ffm"))
            for tt in range(NT):
                d = Y.dma_start(dbg[tt], mf(tt))
                inc(b, d, "d_out", 16)
            wt(b, Y, "d_out")


def p7_out(b, bi):
    nc = b.nc
    T, V, A, G, Y = nc.tensor, nc.vector, nc.scalar, nc.gpsimd, nc.sync
    barrier(b)
    d = Y.dma_start(gbbc(b), b.rowd[3:4, 0:1024].to_broadcast((128, 1024)))
    inc(b, d, "d_bc", 16)
    ldposts = []
    trposts = []
    vposts = []
    for tt in range(NT):
        sslot = (tt % 2) * 1024
        if tt >= 2:
            wt(b, Y, "tmm", trposts[tt - 2])
        x2sem = "d_x2a" if (tt % 2) == 0 else "d_x2b"
        for c in range(NCH):
            d = Y.dma_start(srf(b, sslot + c * 128, 128),
                            b.spillB[c][:, tt * 128:(tt + 1) * 128])
            inc(b, d, x2sem, 16)
        ldposts.append(cnt(b, x2sem))
        wt(b, T, x2sem, ldposts[tt])
        ps = b.psA if (tt % 2) == 0 else b.psB
        if tt >= 2:
            wt(b, T, "vev", vposts[tt - 2])
        for c in range(NCH):
            tr = T.transpose(ps[:, c * 128:(c + 1) * 128],
                             srf(b, sslot + c * 128, 128), b.identf[:])
            inc(b, tr, "tmm", 1)
        trposts.append(cnt(b, "tmm"))
        wt(b, V, "tmm", trposts[tt])
        if tt == 0:
            wt(b, V, "d_bc", cnt(b, "d_bc"))
        p = V.tensor_tensor(out=otsv(b), in0=mixf(b, tt), in1=gbbc(b),
                            op=OP.mult)
        vg(b, p)
        v = V.tensor_tensor(out=otsv(b), in0=otsv(b), in1=ps[:, 0:1024],
                            op=OP.add)
        vposts.append(inc(b, v, "vev", 1))
        wt(b, Y, "vev", vposts[tt])
        d = Y.dma_start(b.out[bi, tt * 128:(tt + 1) * 128, :], otsv(b))
        inc(b, d, "d_out", 16)


def kernel(**inputs):
    xs = np.ascontiguousarray(inputs["L0_attn_x"])
    cs = np.ascontiguousarray(inputs["L0_c"])
    names = {
        "wq": "L0_attn_Wq", "wk": "L0_attn_Wk", "wv": "L0_attn_Wv", "wo": "L0_attn_Wo",
        "bq": "L0_attn_bq", "bk": "L0_attn_bk", "bv": "L0_attn_bv", "bo": "L0_attn_bo",
        "rwt": "L0_ffn_router_w", "rbt": "L0_ffn_router_b",
        "sw1": "L0_ffn_sw1", "sb1": "L0_ffn_sb1", "sw2": "L0_ffn_sw2", "sb2": "L0_ffn_sb2",
        "rw1": "L0_ffn_rw1", "rb1": "L0_ffn_rb1", "rw2": "L0_ffn_rw2", "rb2": "L0_ffn_rb2",
        "wmod": "L0_wmod",
    }
    shared = {k: np.ascontiguousarray(np.asarray(inputs[v], dtype=np.float32))
              for k, v in names.items()}
    b = build(debug=False)
    emit(b)
    in_maps = []
    for c in range(8):
        m = dict(shared)
        m["x"] = np.ascontiguousarray(xs[2 * c:2 * c + 2])
        m["cc"] = np.ascontiguousarray(cs[2 * c:2 * c + 2])
        in_maps.append(m)
    res = run_bass_kernel_spmd(b.nc, in_maps, core_ids=list(range(8)))
    out = np.concatenate([r["out"] for r in res.results], axis=0).astype(np.float32)
    b.ctx.close()
    return out


# version 08
# speedup vs baseline: 1.0221x; 1.0221x over previous
import numpy as np
import concourse.bass as bass
import concourse.mybir as mybir
from concourse.bass_utils import run_bass_kernel_spmd
from concourse.alu_op_type import AluOpType as OP
from contextlib import ExitStack

F32 = mybir.dt.float32
BF16 = mybir.dt.bfloat16
I32 = mybir.dt.int32
AF = mybir.ActivationFunctionType

NB, S, D = 2, 2048, 1024
NH, DH, NPAIR = 16, 64, 8
KEEP = 204
FH, NE = 2048, 8
NCH, NT, NBLK = 8, 16, 4
T0, STEP0, NEG = 1.30, 0.45, -1.0e30
ZINIT, SSTEP = 1.25, 0.40
RPROBE = 5
DVE_ROUNDS = ()
SRING = 11
SRING_XG = (1, 2, 3, 5, 6, 7)
GW = 4

XG_E = 0
XG_G = 4096
XG_IND = 8192
XG_EM = 10240
XG_ETH = 12288


class B:
    pass


def inc(b, instr, name, n=1):
    s = b.sems[name]
    instr.then_inc(s[0], n)
    s[1] += n
    return s[1]


def cnt(b, name):
    return b.sems[name][1]


def dinc(b, eng, name, n=1):
    s = b.sems[name]
    eng.maybe_drain_then_inc((s[0], n))
    s[1] += n
    return s[1]


def wt(b, eng, name, val=None):
    if val is None:
        val = b.sems[name][1]
    if val > 0:
        eng.wait_ge(b.sems[name][0], val)


def vg(b, instr):
    n = inc(b, instr, "vraw", 1)
    b.nc.vector.wait_ge(b.sems["vraw"][0], n)


def barrier(b):
    nc = b.nc
    b.phase += 1
    pairs = [(nc.tensor, "bar_t"), (nc.vector, "bar_v"), (nc.scalar, "bar_a"),
             (nc.gpsimd, "bar_g"), (nc.sync, "bar_y")]
    for eng, nm in pairs:
        dinc(b, eng, nm, 1)
    for eng, nm in pairs:
        for _, nm2 in pairs:
            if nm2 != nm:
                eng.wait_ge(b.sems[nm2][0], b.phase)


def xslot(b, c):
    if c < 4:
        return b.xf[:, c * 2048:(c + 1) * 2048]
    return b.xg[:, (c - 4) * 4096:(c - 4) * 4096 + 4096].bitcast(F32)


def hslot(b, c):
    if c < 4:
        return b.xf[:, c * 2048:c * 2048 + 1024].bitcast(BF16)
    return b.xg[:, (c - 4) * 4096:(c - 4) * 4096 + 2048]


def xgsl(b, off, n):
    return b.xg[:, off:off + n]


def onorm(b, c):
    return b.xf[:, c * 1024:(c + 1) * 1024].bitcast(BF16)


def mixf(b, tt):
    if tt < 8:
        return b.xf[:, tt * 1024:(tt + 1) * 1024]
    return b.xg[:, (tt - 8) * 2048:(tt - 8) * 2048 + 2048].bitcast(F32)


def qbf(b, off, n):
    return b.qb[:, 2 * off:2 * off + 2 * n].bitcast(F32)


def kbf(b, off, n):
    return b.kb[:, 2 * off:2 * off + 2 * n].bitcast(F32)


def srf(b, off, n):
    return b.sr[:, 2 * off:2 * off + 2 * n].bitcast(F32)


def ld3(dst2d, srcdr, inner):
    return (dst2d.rearrange("p (c o) -> p c o", o=inner),
            srcdr.rearrange("(c p) o -> p c o", p=128))


def lnsc(b):
    return srf(b, 0, 2048)


def statsc(b, off, n):
    return b.sr[:, 4096 + off:4096 + off + n]


def wcolv(b, slot):
    return b.sr[:, 4096 + slot * 1024:4096 + slot * 1024 + 1024]


def gbbc(b):
    return srf(b, 2048, 1024)


def otsv(b, n=1024):
    return srf(b, 3072, n)


def build(debug=False):
    b = B()
    b.debug = debug
    nc = bass.Bass("TRN2", target_bir_lowering=False, detect_race_conditions=False)
    b.nc = nc
    ctx = ExitStack()
    b.ctx = ctx
    E = ctx.enter_context

    dri = lambda n, sh: nc.dram_tensor(n, sh, F32, kind="ExternalInput").ap()
    dro = lambda n, sh: nc.dram_tensor(n, sh, F32, kind="ExternalOutput").ap()

    def drt(n, sh):
        kind = "ExternalOutput" if debug else "Internal"
        return nc.dram_tensor(n, sh, F32, kind=kind).ap()

    b.x = dri("x", [NB, S, D])
    b.wq = dri("wq", [D, D]); b.wk = dri("wk", [D, D])
    b.wv = dri("wv", [D, D]); b.wo = dri("wo", [D, D])
    b.bq = dri("bq", [D]); b.bk = dri("bk", [D])
    b.bv = dri("bv", [D]); b.bo = dri("bo", [D])
    b.rwt = dri("rwt", [D, NE]); b.rbt = dri("rbt", [NE])
    b.sw1 = dri("sw1", [D, FH]); b.sb1 = dri("sb1", [FH])
    b.sw2 = dri("sw2", [FH, D]); b.sb2 = dri("sb2", [D])
    b.rw1 = dri("rw1", [NE, D, FH]); b.rb1 = dri("rb1", [NE, FH])
    b.rw2 = dri("rw2", [NE, FH, D]); b.rb2 = dri("rb2", [NE, D])
    b.cc = dri("cc", [NB, D]); b.wmod = dri("wmod", [D, 6 * D])
    b.out = dro("out", [NB, S, D])

    b.spillA = drt("spillA", [NCH, 128, S])
    b.spillB = drt("spillB", [NCH, 128, S])
    b.ethd = nc.dram_tensor("ethd", [2, S], F32, kind="Internal").ap()
    b.rowd = nc.dram_tensor("rowd", [4, S], F32, kind="Internal").ap()
    if debug:
        b.dbg_h = dro("dbg_h", [NCH, 128, S])
        b.dbg_h2 = dro("dbg_h2", [NCH, 128, S])
        b.dbg_ts = dro("dbg_ts", [NB, NPAIR, 128, 32])
        b.dbg_mod = dro("dbg_mod", [NB, 128, 48])
        b.dbg_w = dro("dbg_w", [128, 128])
        b.dbg_mx0 = dro("dbg_mx0", [NT, 128, 1024])
        b.dbg_mx1 = dro("dbg_mx1", [NT, 128, 1024])
        b.dbg_q = dro("dbg_q", [128, S])
        b.dbg_k = dro("dbg_k", [128, S])
        b.dbg_v = dro("dbg_v", [128, 2080])

    b.xf = E(nc.sbuf_tensor("xf", [128, 8192], F32))
    b.xg = E(nc.sbuf_tensor("xg", [128, 16384], BF16))
    b.rwr = E(nc.sbuf_tensor("rwr", [1, 2048], F32))
    b.qb = E(nc.sbuf_tensor("qb", [128, 16384], BF16))
    b.kb = E(nc.sbuf_tensor("kb", [128, 16384], BF16))
    b.vb = E(nc.sbuf_tensor("vb", [128, 16640], BF16))
    b.sr = E(nc.sbuf_tensor("sr", [128, 11264], BF16))
    b.identf = E(nc.sbuf_tensor("identf", [128, 128], F32))
    b.io32 = E(nc.sbuf_tensor("io32", [128, 128], I32))
    b.onesr = E(nc.sbuf_tensor("onesr", [1, 512], BF16))
    b.onesc = E(nc.sbuf_tensor("onesc", [128, 1], BF16))
    b.modv = E(nc.sbuf_tensor("modv", [128, 48], F32))
    b.csil = E(nc.sbuf_tensor("csil", [128, 8], BF16))
    b.craw = E(nc.sbuf_tensor("craw", [128, 8], F32))
    b.bqs = E(nc.sbuf_tensor("bqs", [128, 16], F32))
    b.bvrow = E(nc.sbuf_tensor("bvrow", [1, 1024], BF16))
    b.borow = E(nc.sbuf_tensor("borow", [1, 1024], BF16))
    b.b2row = E(nc.sbuf_tensor("b2row", [2, 1024], BF16))
    b.rbrow = E(nc.sbuf_tensor("rbrow", [1, 8], BF16))
    b.b1ch = E(nc.sbuf_tensor("b1ch", [128, 32], F32))
    b.rwtb = E(nc.sbuf_tensor("rwtb", [128, 64], BF16))
    for nm in ("lo", "hi", "chi", "clo", "pr", "cn", "ts", "s1", "s2", "s3",
               "sm", "sq", "sd", "u1", "u2"):
        setattr(b, "st_" + nm, E(nc.sbuf_tensor("st_" + nm, [128, 32], F32)))
    b.w9 = E(nc.sbuf_tensor("w9", [128, 448], F32))
    b.iota9 = E(nc.sbuf_tensor("iota9", [128, 9], F32))
    b.vsums = E(nc.sbuf_tensor("vsums", [65, 16], F32))
    b.eth = E(nc.sbuf_tensor("eth", [128, 32], F32))
    b.ethT = E(nc.sbuf_tensor("ethT", [32, 128], F32))
    b.rzb = E(nc.sbuf_tensor("rzb", [1, 512], BF16))
    b.gbt = E(nc.sbuf_tensor("gbt", [8, 128], F32))
    b.o1 = E(nc.sbuf_tensor("o1", [64, 512], F32))
    b.wall = E(nc.sbuf_tensor("wall", [128, 128], F32))
    b.prsc = E(nc.sbuf_tensor("prsc", [128, 24], F32))
    b.cst = E(nc.sbuf_tensor("cst", [128, 4], F32))

    b.psA = E(nc.psum_tensor("psA", [128, 2048], F32))
    b.psB = E(nc.psum_tensor("psB", [128, 2048], F32))

    b.sems = {}
    for name in ("bar_t", "bar_v", "bar_a", "bar_g", "bar_y",
                 "d_x", "d_xa", "d_xb", "d_bq", "d_w", "d_wma", "d_wmb", "d_w1", "d_w2",
                 "d_sp", "d_eth", "d_out", "d_row", "d_bc", "d_bce", "d_x2", "d_x2a", "d_x2b",
                 "tmm", "aev", "vev", "sfree", "upd", "prd", "pra",
                 "p2e", "p2g", "resid", "ethx",
                 "ff1", "ff2", "ffg", "ffm", "aux1", "aux2", "aux3", "aux4", "vraw"):
        b.sems[name] = [E(nc.semaphore(name)), 0]
    b.phase = 0
    b.guctr = 0
    b.sfree_post = {}
    b.block = E(nc.Block())
    return b


def emit(b):
    nc = b.nc
    T, V, A, G, Y = nc.tensor, nc.vector, nc.scalar, nc.gpsimd, nc.sync

    d = G.iota(b.io32[:], pattern=[[1, 128]], base=0, channel_multiplier=-1)
    inc(b, d, "aux1", 1)
    iv = b.w9[:, 0:9].bitcast(I32)
    d = G.iota(iv, pattern=[[1, 9]], base=0, channel_multiplier=0)
    inc(b, d, "aux1", 1)
    wt(b, V, "aux1", 2)
    V.tensor_scalar(out=b.identf[:], in0=b.io32[:], scalar1=0.0, scalar2=None, op0=OP.is_equal)
    V.tensor_copy(b.iota9[:], iv)
    V.memset(b.onesr[:], 1.0)
    V.memset(b.onesc[:], 1.0)
    V.memset(b.cst[:, 0:1], 1e-5)
    dinc(b, V, "aux2", 1)

    d = G.dma_start(b.bvrow[:], b.bv[None, :]); inc(b, d, "d_w", 16)
    d = G.dma_start(b.borow[:], b.bo[None, :]); inc(b, d, "d_w", 16)
    d = G.dma_start(b.rbrow[:], b.rbt[None, :]); inc(b, d, "d_w", 16)
    _d, _s = ld3(b.rwtb[:], b.rwt, NE)
    d = G.dma_start(_d, _s); inc(b, d, "d_w", 16)
    d = Y.dma_start(b.bqs[:, 0:8], b.bq.rearrange("(c p) -> p c", p=128), allow_slow_non_contiguous=True); inc(b, d, "d_bq", 16)
    d = Y.dma_start(b.bqs[:, 8:16], b.bk.rearrange("(c p) -> p c", p=128), allow_slow_non_contiguous=True); inc(b, d, "d_bq", 16)

    import os
    stage = int(os.environ.get("KSTAGE", "8"))
    phases = [p0_load_mod, p1_ln1, p2_qkv, p3_attention, p4_ln2, p5_router, p6_ffn, p7_out]
    for bi in range(NB):
        for pi, ph in enumerate(phases[:stage]):
            ph(b, bi)

    wt(b, Y, "d_out")


def dump(b, src_f32_ap, dst_dram_ap):
    nc = b.nc
    V, Y = nc.vector, nc.sync
    d = Y.dma_start(dst_dram_ap, src_f32_ap)
    n = inc(b, d, "d_out", 16)
    wt(b, Y, "d_out", n)


def p0_load_mod(b, bi):
    nc = b.nc
    T, V, A, G, Y = nc.tensor, nc.vector, nc.scalar, nc.gpsimd, nc.sync
    barrier(b)

    d = Y.dma_start(b.craw[:], b.cc[bi].rearrange("(c p) -> p c", p=128), allow_slow_non_contiguous=True)
    inc(b, d, "d_row", 16)
    wt(b, A, "d_row")
    A.activation(out=b.craw[:], in_=b.craw[:], func=AF.Silu)
    dinc(b, A, "aux3", 1)
    wt(b, V, "aux3")
    V.tensor_copy(b.csil[:], b.craw[:])
    dinc(b, V, "aux3", 1)

    wt(b, T, "aux2", 1)
    tposted = []
    for t in range(NT):
        buf = srf(b, (t % 2) * 1024, 1024)
        if t >= 2:
            wt(b, Y, "tmm", tposted[t - 2])
        xsem = "d_xa" if (t % 2) == 0 else "d_xb"
        d = Y.dma_start(buf, b.x[bi, t * 128:(t + 1) * 128, :])
        nld = inc(b, d, xsem, 16)
        wt(b, T, xsem, nld)
        if t >= 2:
            wt(b, T, "aev", cnt(b, "aev") - NCH)
        ps = b.psA if (t % 2) == 0 else b.psB
        for c in range(NCH):
            tr = T.transpose(ps[:, c * 128:(c + 1) * 128], buf[:, c * 128:(c + 1) * 128],
                             b.identf[:])
            inc(b, tr, "tmm", 1)
        tposted.append(cnt(b, "tmm"))
        wt(b, A, "tmm")
        wt(b, V, "tmm")
        for c in range(NCH):
            dstx = xslot(b, c)[:, t * 128:(t + 1) * 128]
            if c < 4:
                ev = A.activation(out=dstx, in_=ps[:, c * 128:(c + 1) * 128], func=AF.Copy)
            else:
                ev = V.tensor_copy(dstx, ps[:, c * 128:(c + 1) * 128])
            inc(b, ev, "aev", 1)

    wt(b, T, "aev")
    wt(b, T, "aux3", cnt(b, "aux3"))
    wcons = []
    for oc in range(48):
        w = wcolv(b, oc % 2)
        if oc >= 2:
            wt(b, G, "tmm", wcons[oc - 2])
        _d, _s = ld3(w, b.wmod[:, oc * 128:(oc + 1) * 128], 128)
        d = G.dma_start(_d, _s)
        wsem = "d_wma" if (oc % 2) == 0 else "d_wmb"
        ndw = inc(b, d, wsem, 16)
        wt(b, T, wsem, ndw)
        for kc in range(NCH):
            mm = T.matmul(b.psA[:, 1024 + oc:1024 + oc + 1], w[:, kc * 128:(kc + 1) * 128],
                          b.csil[:, kc:kc + 1], start=(kc == 0), stop=(kc == NCH - 1),
                          skip_group_check=True)
        inc(b, mm, "tmm", 1)
        wcons.append(cnt(b, "tmm"))
    wt(b, A, "tmm")
    ev = A.activation(out=b.modv[:], in_=b.psA[:, 1024:1072], func=AF.Copy)
    inc(b, ev, "aev", 1)
    wt(b, V, "aev")
    V.tensor_scalar(out=b.modv[:, 8:16], in0=b.modv[:, 8:16], scalar1=1.0, scalar2=None, op0=OP.add)
    V.tensor_scalar(out=b.modv[:, 32:40], in0=b.modv[:, 32:40], scalar1=1.0, scalar2=None, op0=OP.add)
    for col in (16, 40):
        sl = b.modv[:, col:col + 8]
        p = V.tensor_scalar(out=b.st_s1[:, 0:8], in0=sl, scalar1=0.0, scalar2=2.0,
                            op0=OP.is_ge, op1=OP.mult)
        vg(b, p)
        p = V.tensor_scalar(out=b.st_s1[:, 0:8], in0=b.st_s1[:, 0:8], scalar1=-1.0,
                            scalar2=None, op0=OP.add)
        p = V.tensor_scalar(out=b.st_s2[:, 0:8], in0=sl, scalar1=-1.0, scalar2=None,
                            op0=OP.mult)
        vg(b, p)
        p = V.tensor_tensor(out=b.st_s2[:, 0:8], in0=sl, in1=b.st_s2[:, 0:8], op=OP.max)
        vg(b, p)
        p = V.tensor_scalar(out=b.st_s2[:, 0:8], in0=b.st_s2[:, 0:8], scalar1=0.25,
                            scalar2=None, op0=OP.max)
        vg(b, p)
        p = V.tensor_tensor(out=sl, in0=b.st_s1[:, 0:8], in1=b.st_s2[:, 0:8], op=OP.mult)
        vg(b, p)
    dinc(b, V, "aux4", 1)

    wt(b, T, "aux4", cnt(b, "aux4"))
    tr = T.transpose(b.psB[0:8, 0:128], b.modv[:, 40:48], b.identf[:])
    ntm = inc(b, tr, "tmm", 1)
    wt(b, V, "tmm", ntm)
    V.tensor_copy(b.gbt[:], b.psB[0:8, 0:128])
    nvv = dinc(b, V, "vev", 1)
    wt(b, Y, "vev", nvv)
    d = Y.dma_start(b.rowd[3:4, 0:1024].rearrange("a (c q) -> (a c) q", q=128), b.gbt[:])
    ndr = inc(b, d, "d_row", 16)
    wt(b, Y, "d_row", ndr)
    if b.debug:
        dump(b, b.modv[:], b.dbg_mod[bi, :, :])


def ln_phase(b, bi, src_resident, bchome, ca_col, sa_col, dst_xb, spill):
    nc = b.nc
    T, V, A, G, Y = nc.tensor, nc.vector, nc.scalar, nc.gpsimd, nc.sync
    home = qbf if bchome == "qb" else kbf
    stream = not src_resident
    vdone = []

    for c in range(NCH):
        if stream:
            xcb = b.xf[:, (c % 2) * 2048:(c % 2) * 2048 + 2048]
            x2sem = "d_x2a" if (c % 2) == 0 else "d_x2b"
            if c >= 2:
                wt(b, Y, "vev", vdone[c - 2])
            d = Y.dma_start(xcb, b.spillB[c])
            nld = inc(b, d, x2sem, 16)
            wt(b, V, x2sem, nld)
        else:
            xcb = xslot(b, c)
        V.tensor_copy(statsc(b, 0, 2048), xcb)
        V.tensor_tensor(out=statsc(b, 2048, 2048), in0=xcb, in1=xcb, op=OP.mult)
        nvv = dinc(b, V, "vev", 1)
        vdone.append(nvv)
        wt(b, T, "vev", nvv)
        for blk in range(NBLK):
            mm = T.matmul(b.psA[0:1, blk * 512:(blk + 1) * 512], b.onesc[:],
                          statsc(b, blk * 512, 512),
                          start=(c == 0), stop=(c == NCH - 1), skip_group_check=True)
            mm = T.matmul(b.psB[0:1, blk * 512:(blk + 1) * 512], b.onesc[:],
                          statsc(b, 2048 + blk * 512, 512),
                          start=(c == 0), stop=(c == NCH - 1), skip_group_check=True)
        inc(b, mm, "tmm", 1)

    wt(b, V, "tmm")
    L1 = lnsc(b)[0:1, :]
    L2 = srf(b, 3072, 2048)[0:1, :]
    V.tensor_scalar(out=L1, in0=b.psA[0:1, :], scalar1=1.0 / D, scalar2=None, op0=OP.mult)
    V.tensor_scalar(out=L2, in0=b.psB[0:1, :], scalar1=1.0 / D, scalar2=None, op0=OP.mult)
    nvv = dinc(b, V, "vev", 1)
    wt(b, Y, "vev", nvv)
    d = Y.dma_start(b.rowd[0:1, :], L1)
    ndr = inc(b, d, "d_row", 16)
    wt(b, V, "d_row", ndr)
    V.tensor_tensor(out=b.rwr[0:1, :], in0=L1, in1=L1, op=OP.mult)
    V.tensor_tensor(out=b.rwr[0:1, :], in0=L2, in1=b.rwr[0:1, :], op=OP.subtract)
    nvv = dinc(b, V, "vev", 1)
    wt(b, A, "vev", nvv)
    A.activation(out=b.rwr[0:1, :], in_=b.rwr[0:1, :], func=AF.Sqrt,
                 bias=b.cst[0:1, 0:1], scale=1.0)
    nav = dinc(b, A, "aev", 1)
    wt(b, V, "aev", nav)
    V.reciprocal(out=b.rwr[0:1, :], in_=b.rwr[0:1, :])
    nvv = dinc(b, V, "vev", 1)
    wt(b, Y, "vev", nvv)
    d = Y.dma_start(b.rowd[1:2, :], b.rwr[0:1, :])
    ndr = inc(b, d, "d_row", 16)
    wt(b, Y, "d_row", ndr)
    d = Y.dma_start(home(b, 0, 2048), b.rowd[0:1, :].to_broadcast((128, 2048)))
    inc(b, d, "d_bc", 16)
    d = Y.dma_start(home(b, 2048, 2048), b.rowd[1:2, :].to_broadcast((128, 2048)))
    nbc = inc(b, d, "d_bc", 16)
    wt(b, V, "d_bc", nbc)
    wt(b, V, "aux4", cnt(b, "aux4"))

    for c in range(NCH):
        if stream:
            xcb = b.xf[:, (c % 2) * 2048:(c % 2) * 2048 + 2048]
            x2sem = "d_x2a" if (c % 2) == 0 else "d_x2b"
            wt(b, Y, "vev", vdone[NCH + c - 2] if c >= 2 else vdone[NCH - 2 + c])
            d = Y.dma_start(xcb, b.spillB[c])
            nld = inc(b, d, x2sem, 16)
            wt(b, V, x2sem, nld)
        else:
            xcb = xslot(b, c)
            if c >= 1:
                wt(b, Y, "vev", vdone[-1])
            d = Y.dma_start(spill[c], xcb)
            nsp = inc(b, d, "d_sp", 16)
        V.tensor_tensor(out=lnsc(b), in0=xcb, in1=home(b, 0, 2048), op=OP.subtract)
        V.tensor_tensor(out=lnsc(b), in0=lnsc(b), in1=home(b, 2048, 2048), op=OP.mult)
        if not stream:
            wt(b, V, "d_sp", nsp)
        dstap = (hslot(b, c) if dst_xb else b.qb[:, c * 2048:(c + 1) * 2048])
        V.tensor_scalar(out=dstap, in0=lnsc(b),
                        scalar1=b.modv[:, ca_col + c:ca_col + c + 1],
                        scalar2=b.modv[:, sa_col + c:sa_col + c + 1],
                        op0=OP.mult, op1=OP.add)
        nvv = dinc(b, V, "vev", 1)
        vdone.append(nvv)


def p1_ln1(b, bi):
    barrier(b)
    ln_phase(b, bi, src_resident=True, bchome="qb", ca_col=8, sa_col=0, dst_xb=True,
             spill=b.spillA)
    if b.debug:
        nc = b.nc
        V, Y = nc.vector, nc.sync
        for c in range(NCH):
            wt(b, V, "vev")
            V.tensor_copy(lnsc(b), hslot(b, c))
            n = dinc(b, V, "vev", 1)
            wt(b, Y, "vev", n)
            dump(b, lnsc(b), b.dbg_h[c])
            wt(b, V, "d_out")


def p4_ln2(b, bi):
    barrier(b)
    ln_phase(b, bi, src_resident=False, bchome="kb", ca_col=32, sa_col=24, dst_xb=False,
             spill=None)
    if b.debug:
        nc = b.nc
        V, Y = nc.vector, nc.sync
        for c in range(NCH):
            wt(b, V, "vev")
            V.tensor_copy(lnsc(b), b.qb[:, c * 2048:(c + 1) * 2048])
            n = dinc(b, V, "vev", 1)
            wt(b, Y, "vev", n)
            dump(b, lnsc(b), b.dbg_h2[c])
            wt(b, V, "d_out")


def p2_qkv(b, bi):
    nc = b.nc
    T, V, A, G, Y = nc.tensor, nc.vector, nc.scalar, nc.gpsimd, nc.sync
    barrier(b)
    wb = b.sr[:, 0:8192]
    h = lambda c: hslot(b, c)

    wt(b, A, "d_bq", 32)
    for wi, (wdr, dst, bcol) in enumerate([(b.wq, b.qb, 0), (b.wk, b.kb, 8)]):
        if wi > 0:
            wt(b, G, "tmm", cnt(b, "tmm"))
        _d, _s = ld3(wb, wdr, 1024)
        d = G.dma_start(_d, _s)
        ndw = inc(b, d, "d_w", 16)
        wt(b, T, "d_w", ndw)
        for i in range(NCH * NBLK):
            oc, blk = divmod(i, NBLK)
            pss = b.psA if (i % 2) == 0 else b.psB
            bank = (i // 2) % 4
            ps = pss[:, bank * 512:(bank + 1) * 512]
            if i >= 8:
                wt(b, T, "aev", cnt(b, "aev") - 7)
            for kc in range(NCH):
                mm = T.matmul(ps, wb[:, kc * 1024 + oc * 128:kc * 1024 + oc * 128 + 128],
                              h(kc)[:, blk * 512:(blk + 1) * 512],
                              start=(kc == 0), stop=(kc == NCH - 1))
            ntm = inc(b, mm, "tmm", 1)
            wt(b, A, "tmm", ntm)
            ev = A.activation(out=dst[:, oc * 2048 + blk * 512:oc * 2048 + blk * 512 + 512],
                              in_=ps, func=AF.Identity, bias=b.bqs[:, bcol + oc:bcol + oc + 1])
            inc(b, ev, "aev", 1)

    wt(b, G, "tmm", cnt(b, "tmm"))
    _d, _s = ld3(wb, b.wv, 1024)
    d = G.dma_start(_d, _s)
    ndw = inc(b, d, "d_w", 16)
    wt(b, V, "aev", cnt(b, "aev"))
    vv = b.vb[:].rearrange("p (t hh j) -> p t hh j", hh=16, j=65)
    V.memset(vv[:, :, :, 64:65], 1.0)
    dinc(b, V, "vev", 1)
    wt(b, T, "d_w", ndw)
    wt(b, A, "vev", cnt(b, "vev"))
    for i in range(NT * 2):
        tt, half = divmod(i, 2)
        pss = b.psA if (i % 2) == 0 else b.psB
        bank = (i // 2) % 4
        ps = pss[:, bank * 512:(bank + 1) * 512]
        if i >= 8:
            wt(b, T, "aev", cnt(b, "aev") - 7)
        for kc in range(NCH):
            mm = T.matmul(ps, h(kc)[:, tt * 128:(tt + 1) * 128],
                          wb[:, kc * 1024 + half * 512:kc * 1024 + half * 512 + 512],
                          start=(kc == 0), stop=False)
        mm = T.matmul(ps, b.onesr[0:1, 0:128],
                      b.bvrow[0:1, half * 512:(half + 1) * 512], start=False, stop=True)
        ntm = inc(b, mm, "tmm", 1)
        wt(b, A, "tmm", ntm)
        dstv = (b.vb[:, tt * 1040 + half * 520:tt * 1040 + half * 520 + 520]
                .rearrange("p (hh j) -> p hh j", j=65)[:, :, 0:64])
        ev = A.activation(out=dstv, in_=ps.rearrange("p (hh j) -> p hh j", j=64), func=AF.Copy)
        inc(b, ev, "aev", 1)

    if b.debug:
        wt(b, V, "aev")
        V.tensor_copy(otsv(b, 2048), b.qb[:, 0:2048])
        n = dinc(b, V, "vev", 1)
        wt(b, Y, "vev", n); dump(b, otsv(b, 2048), b.dbg_q); wt(b, V, "d_out")
        V.tensor_copy(otsv(b, 2048), b.kb[:, 0:2048])
        n = dinc(b, V, "vev", 1)
        wt(b, Y, "vev", n); dump(b, otsv(b, 2048), b.dbg_k); wt(b, V, "d_out")
        V.tensor_copy(otsv(b, 2048), b.vb[:, 0:2048])
        n = dinc(b, V, "vev", 1)
        wt(b, Y, "vev", n); dump(b, otsv(b, 2048), b.dbg_v[:, 0:2048]); wt(b, V, "d_out")


def p3_attention(b, bi):
    nc = b.nc
    T, V, A, G, Y = nc.tensor, nc.vector, nc.scalar, nc.gpsimd, nc.sync
    barrier(b)

    wt(b, T, "aux2", 1)
    for hh in range(NH):
        for c in range(NT):
            mm = T.matmul(b.psA[0:65, hh:hh + 1],
                          b.vb[:, c * 1040 + hh * 65:c * 1040 + hh * 65 + 65],
                          b.onesc[:], start=(c == 0), stop=(c == NT - 1),
                          skip_group_check=True)
    ntm = inc(b, mm, "tmm", 1)
    wt(b, A, "tmm", ntm)
    ev = A.activation(out=b.vsums[:], in_=b.psA[0:65, 0:16], func=AF.Copy)
    nav = inc(b, ev, "aev", 1)
    wt(b, T, "aev", nav)

    import os as _os
    for pair in range(int(_os.environ.get("KPAIRS", str(NPAIR)))):
        attn_pair(b, bi, pair)

    wb = b.vb[:, 0:8192]
    wt(b, G, "sfree", cnt(b, "sfree"))
    _d, _s = ld3(wb, b.wo, 1024)
    d = G.dma_start(_d, _s)
    ndw = inc(b, d, "d_w", 16)
    wt(b, T, "d_w", ndw)
    on = lambda c: onorm(b, c)
    wt(b, Y, "sfree", cnt(b, "sfree"))
    for oc in range(NCH):
        xcb = srf(b, (oc % 2) * 2048, 2048)
        x2sem = "d_x2a" if (oc % 2) == 0 else "d_x2b"
        if oc >= 2:
            wt(b, Y, "resid", cnt(b, "resid") - NBLK)
            wt(b, Y, "d_sp", cnt(b, "d_sp"))
        d = Y.dma_start(xcb, b.spillA[oc])
        nld = inc(b, d, x2sem, 16)
        for blk in range(NBLK):
            i = oc * NBLK + blk
            pss = b.psA if (i % 2) == 0 else b.psB
            bank = (i // 2) % 4
            ps = pss[:, bank * 512:(bank + 1) * 512]
            if i >= 8:
                wt(b, T, "resid", cnt(b, "resid") - 7)
            for kc in range(NCH):
                mm = T.matmul(ps, wb[:, kc * 1024 + oc * 128:kc * 1024 + oc * 128 + 128],
                              on(kc)[:, blk * 512:(blk + 1) * 512],
                              start=(kc == 0), stop=False)
            mm = T.matmul(ps, b.borow[0:1, oc * 128:(oc + 1) * 128],
                          b.onesr[0:1, 0:512], start=False, stop=True)
            ntm = inc(b, mm, "tmm", 1)
            wt(b, V, "tmm", ntm)
            wt(b, V, x2sem, nld)
            st = V.scalar_tensor_tensor(out=xcb[:, blk * 512:(blk + 1) * 512],
                                        in0=ps, scalar=b.modv[:, 16 + oc:16 + oc + 1],
                                        in1=xcb[:, blk * 512:(blk + 1) * 512],
                                        op0=OP.mult, op1=OP.add)
            inc(b, st, "resid", 1)
        wt(b, Y, "resid", cnt(b, "resid"))
        d = Y.dma_start(b.spillB[oc], xcb)
        inc(b, d, "d_sp", 16)
    wt(b, Y, "d_sp")


def attn_pair(b, bi, pair):
    nc = b.nc
    T, V, A, G, Y = nc.tensor, nc.vector, nc.scalar, nc.gpsimd, nc.sync
    hA, hB = 2 * pair, 2 * pair + 1

    def qsl(h, c0, n):
        p0 = (h % 2) * 64
        return b.qb[p0:p0 + 64, (h // 2) * 2048 + c0:(h // 2) * 2048 + c0 + n]

    def ksl(h, c0, n):
        p0 = (h % 2) * 64
        return b.kb[p0:p0 + 64, (h // 2) * 2048 + c0:(h // 2) * 2048 + c0 + n]

    def s16ap(u):
        s = (b.guctr + u) % SRING
        if s < 5:
            return b.sr[:, s * 2048:(s + 1) * 2048]
        return xgsl(b, SRING_XG[s - 5] * 2048, 2048)

    wt(b, V, "ethx", cnt(b, "ethx"))
    V.memset(b.st_lo[:], NEG)
    V.memset(b.st_hi[:], 1.0e30)
    V.memset(b.st_chi[:], 0.0)
    V.memset(b.st_clo[:], float(S))
    p = V.memset(b.st_pr[:], T0)
    vg(b, p)
    dinc(b, V, "upd", 1)

    wt(b, T, "p2e", cnt(b, "p2e"))
    evposts = {}
    def emit_probes(units, r, upd_need):
        dve = r in DVE_ROUNDS
        for u in units:
            if dve:
                wt(b, V, "aev", evposts[u])
                wt(b, V, "upd", upd_need)
                pr = V.tensor_scalar(out=xgsl(b, XG_IND, 2048),
                                     in0=s16ap(u), scalar1=b.st_pr[:, u:u + 1],
                                     scalar2=0.0, op0=OP.is_ge, op1=OP.add,
                                     accum_out=b.st_cn[:, u:u + 1])
                inc(b, pr, "prd", 1)
            else:
                wt(b, A, "aev", evposts[u])
                wt(b, A, "upd", upd_need)
                pr = A.activation(out=xgsl(b, XG_E, 2048),
                                  in_=s16ap(u), func=AF.Sign,
                                  bias=b.st_pr[:, u:u + 1], scale=-1.0,
                                  accum_out=b.st_cn[:, u:u + 1])
                inc(b, pr, "pra", 1)
        return cnt(b, "prd" if dve else "pra")

    def emit_update(units, r, npr):
        dve = r in DVE_ROUNDS
        step_r = STEP0 * (2.0 ** r)
        u0, n = units[0], len(units)
        L = lambda t: t[:, u0:u0 + n]
        if dve:
            wt(b, V, "prd", npr)
        else:
            wt(b, V, "pra", npr)
            p = V.tensor_scalar(out=L(b.st_cn), in0=L(b.st_cn), scalar1=-0.5,
                                scalar2=1024.0, op0=OP.mult, op1=OP.add)
            vg(b, p)
        p = V.tensor_scalar(out=L(b.st_s1), in0=L(b.st_cn), scalar1=float(KEEP),
                            scalar2=None, op0=OP.is_ge)
        vg(b, p)
        m1 = L(b.st_s1).bitcast(I32)
        V.copy_predicated(out=L(b.st_lo), mask=m1, data=L(b.st_pr))
        V.copy_predicated(out=L(b.st_clo), mask=m1, data=L(b.st_cn))
        p = V.tensor_scalar(out=L(b.st_s2), in0=L(b.st_s1), scalar1=-1.0, scalar2=1.0,
                            op0=OP.mult, op1=OP.add)
        vg(b, p)
        m2 = L(b.st_s2).bitcast(I32)
        V.copy_predicated(out=L(b.st_hi), mask=m2, data=L(b.st_pr))
        p = V.copy_predicated(out=L(b.st_chi), mask=m2, data=L(b.st_cn))
        if r < RPROBE - 1:
            vg(b, p)
            V.tensor_scalar(out=L(b.st_s1), in0=L(b.st_lo), scalar1=-1.0e29, scalar2=None,
                            op0=OP.is_le)
            V.tensor_scalar(out=L(b.st_s2), in0=L(b.st_hi), scalar1=1.0e29, scalar2=None,
                            op0=OP.is_ge)
            p = V.tensor_tensor(out=L(b.st_u1), in0=L(b.st_hi), in1=L(b.st_lo),
                                op=OP.subtract)
            p = V.tensor_tensor(out=L(b.st_u2), in0=L(b.st_clo), in1=L(b.st_chi),
                                op=OP.subtract)
            vg(b, p)
            p = V.reciprocal(out=L(b.st_u2), in_=L(b.st_u2))
            p2 = V.tensor_scalar(out=L(b.st_cn), in0=L(b.st_cn), scalar1=-float(KEEP),
                                 scalar2=None, op0=OP.add)
            vg(b, p2)
            p = V.tensor_tensor(out=L(b.st_u2), in0=L(b.st_u2), in1=L(b.st_cn), op=OP.mult)
            vg(b, p)
            p = V.tensor_tensor(out=L(b.st_u2), in0=L(b.st_u2), in1=L(b.st_u1), op=OP.mult)
            vg(b, p)
            p = V.tensor_tensor(out=L(b.st_u2), in0=L(b.st_pr), in1=L(b.st_u2), op=OP.add)
            p2 = V.tensor_scalar(out=L(b.st_u1), in0=L(b.st_u1), scalar1=0.05, scalar2=None,
                                 op0=OP.mult)
            vg(b, p2)
            p = V.tensor_tensor(out=L(b.st_s3), in0=L(b.st_lo), in1=L(b.st_u1), op=OP.add)
            vg(b, p)
            p = V.tensor_tensor(out=L(b.st_u2), in0=L(b.st_u2), in1=L(b.st_s3), op=OP.max)
            p2 = V.tensor_tensor(out=L(b.st_s3), in0=L(b.st_hi), in1=L(b.st_u1),
                                 op=OP.subtract)
            vg(b, p2)
            p = V.tensor_tensor(out=L(b.st_pr), in0=L(b.st_u2), in1=L(b.st_s3), op=OP.min)
            p2 = V.tensor_scalar(out=L(b.st_u1), in0=L(b.st_sd),
                                 scalar1=float(SSTEP * (2.0 ** r)), scalar2=None, op0=OP.mult)
            vg(b, p2)
            p = V.tensor_tensor(out=L(b.st_s3), in0=L(b.st_hi), in1=L(b.st_u1),
                                op=OP.subtract)
            vg(b, p)
            V.copy_predicated(out=L(b.st_pr), mask=L(b.st_s1).bitcast(I32), data=L(b.st_s3))
            p = V.tensor_tensor(out=L(b.st_s3), in0=L(b.st_lo), in1=L(b.st_u1), op=OP.add)
            vg(b, p)
            p = V.copy_predicated(out=L(b.st_pr), mask=L(b.st_s2).bitcast(I32),
                                  data=L(b.st_s3))
            vg(b, p)
        else:
            vg(b, p)
        dinc(b, V, "upd", 1)
        return cnt(b, "upd")

    def emit_fixup_masks(units8):
        ps = []
        for u in units8:
            ps.append(V.scalar_tensor_tensor(out=s16ap(u), in0=s16ap(u),
                                             scalar=b.st_hi[:, u:u + 1], in1=s16ap(u),
                                             op0=OP.is_lt, op1=OP.mult))
        vg(b, ps[-1])
        for u in units8:
            mx = V.max(out=b.w9[:, u * 9:u * 9 + 8], in_=s16ap(u))
            b.sfree_post[b.guctr + u] = inc(b, mx, "sfree", 1)
            V.tensor_copy(b.w9[:, u * 9 + 8:u * 9 + 9], b.st_lo[:, u:u + 1])

    def emit_fixup_picks(units8):
        u0s = units8[0]
        Lp = lambda t: t[:, u0s:u0s + 2 * GW]
        scr = lambda k: b.w9[:, 288 + k * 9:288 + k * 9 + 9]
        p = V.tensor_scalar(out=Lp(b.st_s1), in0=Lp(b.st_chi), scalar1=-1.0, scalar2=203.0,
                            op0=OP.mult, op1=OP.add)
        vg(b, p)
        V.tensor_scalar(out=Lp(b.st_u2), in0=Lp(b.st_s1), scalar1=1.0, scalar2=None,
                        op0=OP.add)
        for i, u in enumerate(units8):
            p = V.tensor_scalar(out=scr(2 * i), in0=b.iota9[:],
                                scalar1=b.st_s1[:, u:u + 1], scalar2=None, op0=OP.is_equal)
        vg(b, p)
        for i, u in enumerate(units8):
            V.tensor_scalar(out=scr(2 * i + 1), in0=b.iota9[:],
                            scalar1=b.st_u2[:, u:u + 1], scalar2=None, op0=OP.is_equal)
            p = V.tensor_tensor(out=scr(2 * i), in0=b.w9[:, u * 9:u * 9 + 9],
                                in1=scr(2 * i), op=OP.mult)
        vg(b, p)
        for i, u in enumerate(units8):
            V.tensor_tensor(out=scr(2 * i + 1), in0=b.w9[:, u * 9:u * 9 + 9],
                            in1=scr(2 * i + 1), op=OP.mult)
            p = V.tensor_scalar(out=scr(2 * i), in0=scr(2 * i), scalar1=0.0, scalar2=0.0,
                                op0=OP.add, op1=OP.add, accum_out=b.st_s2[:, u:u + 1])
        vg(b, p)
        for i, u in enumerate(units8):
            p = V.tensor_scalar(out=scr(2 * i + 1), in0=scr(2 * i + 1), scalar1=0.0,
                                scalar2=0.0, op0=OP.add, op1=OP.add,
                                accum_out=b.st_s3[:, u:u + 1])
        vg(b, p)
        V.tensor_tensor(out=Lp(b.st_s2), in0=Lp(b.st_s2), in1=Lp(b.st_s3), op=OP.add)
        V.tensor_scalar(out=Lp(b.st_u1), in0=Lp(b.st_s1), scalar1=6.5, scalar2=None,
                        op0=OP.is_le)
        V.tensor_scalar(out=Lp(b.st_sm), in0=Lp(b.st_s1), scalar1=6.5, scalar2=None,
                        op0=OP.is_gt)
        p = V.tensor_scalar(out=Lp(b.st_sq), in0=Lp(b.st_clo), scalar1=204.0, scalar2=None,
                            op0=OP.is_equal)
        vg(b, p)
        V.tensor_scalar(out=Lp(b.st_s2), in0=Lp(b.st_s2), scalar1=0.5, scalar2=None,
                        op0=OP.mult)
        p = V.tensor_tensor(out=Lp(b.st_sm), in0=Lp(b.st_sm), in1=Lp(b.st_sq), op=OP.mult)
        vg(b, p)
        V.tensor_tensor(out=Lp(b.st_u1), in0=Lp(b.st_u1), in1=Lp(b.st_sm), op=OP.add)
        p = V.tensor_scalar(out=Lp(b.st_sd), in0=Lp(b.st_s1), scalar1=-0.5, scalar2=None,
                            op0=OP.is_ge)
        vg(b, p)
        V.tensor_tensor(out=Lp(b.st_u1), in0=Lp(b.st_u1), in1=Lp(b.st_sd), op=OP.mult)
        p = V.tensor_copy(Lp(b.st_ts), Lp(b.st_lo))
        vg(b, p)
        p = V.copy_predicated(out=Lp(b.st_ts), mask=Lp(b.st_u1).bitcast(I32),
                              data=Lp(b.st_s2))
        vg(b, p)

    upd0 = cnt(b, "upd")
    for gp in range(32 // (2 * GW)):
        for j in range(gp * GW, (gp + 1) * GW):
            wt(b, T, "aev", cnt(b, "aev"))
            for blk in range(NBLK):
                mm = T.matmul(b.psA[:, blk * 512:(blk + 1) * 512],
                              qsl(hA, j * 128, 128), ksl(hA, blk * 512, 512),
                              start=True, stop=True, tile_position=(0, 0))
                mm = T.matmul(b.psB[:, blk * 512:(blk + 1) * 512],
                              qsl(hB, j * 128, 128), ksl(hB, blk * 512, 512),
                              start=True, stop=True, tile_position=(64, 0))
            ntm = inc(b, mm, "tmm", 1)
            wt(b, A, "tmm", ntm)
            for hi_, ps in ((0, b.psA), (1, b.psB)):
                u = 2 * j + hi_
                prev = b.sfree_post.get(b.guctr + u - SRING)
                if prev is not None:
                    wt(b, A, "sfree", prev)
                ev = A.activation(out=s16ap(u), in_=ps[:], func=AF.Copy, scale=0.125)
                evposts[u] = inc(b, ev, "aev", 1)
                sqp = A.activation(out=xgsl(b, XG_E, 2048), in_=ps[:], func=AF.Square,
                                   scale=0.125, accum_out=b.st_sq[:, u:u + 1])
                inc(b, sqp, "aev", 1)
        u0s = 2 * gp * GW
        Ls = lambda t: t[:, u0s:u0s + 2 * GW]
        wt(b, V, "aev", cnt(b, "aev"))
        p = V.tensor_scalar(out=Ls(b.st_sq), in0=Ls(b.st_sq), scalar1=1.0 / S, scalar2=None,
                            op0=OP.mult)
        vg(b, p)
        nvv = dinc(b, V, "vev", 1)
        wt(b, A, "vev", nvv)
        sq = A.activation(out=Ls(b.st_sd), in_=Ls(b.st_sq), func=AF.Sqrt)
        nav = inc(b, sq, "aev", 1)
        wt(b, V, "aev", nav)
        p = V.tensor_scalar(out=Ls(b.st_pr), in0=Ls(b.st_sd), scalar1=ZINIT, scalar2=None,
                            op0=OP.mult)
        vg(b, p)
        upd0gp = dinc(b, V, "upd", 1)
        halves = [list(range((2 * gp + h) * GW, (2 * gp + h + 1) * GW)) for h in (0, 1)]
        upd_post = [upd0gp, upd0gp]
        for r in range(RPROBE):
            npr0 = emit_probes(halves[0], r, upd_post[0])
            npr1 = emit_probes(halves[1], r, upd_post[1])
            upd_post[0] = emit_update(halves[0], r, npr0)
            upd_post[1] = emit_update(halves[1], r, npr1)
        emit_fixup_masks(halves[0] + halves[1])
        emit_fixup_picks(halves[0] + halves[1])

    b.guctr += 32
    npick = dinc(b, V, "ethx", 1)
    if b.debug:
        wt(b, Y, "ethx", npick)
        dump(b, b.st_ts[:], b.dbg_ts[bi, pair])

    import os as _os
    if _os.environ.get("KNOP2", "0") == "1":
        return
    wt(b, A, "ethx", npick)
    A.activation(out=b.eth[:].rearrange("p (two j) -> p two j", j=16),
                 in_=b.st_ts[:].rearrange("p (j two) -> p two j", two=2), func=AF.Exp)
    nav = dinc(b, A, "ethx", 1)
    wt(b, T, "ethx", cnt(b, "ethx"))
    wt(b, T, "aev", cnt(b, "aev"))
    tr = T.transpose(b.psB[0:32, 0:128], b.eth[:], b.identf[:])
    ntm = inc(b, tr, "tmm", 1)
    wt(b, A, "tmm", ntm)
    ev = A.activation(out=b.ethT[:], in_=b.psB[0:32, 0:128], func=AF.Copy)
    nav = inc(b, ev, "aev", 1)
    wt(b, Y, "aev", nav)
    for hi_ in range(2):
        d = Y.dma_start(b.ethd[hi_:hi_ + 1, :].rearrange("a (t q) -> (a t) q", q=128),
                        b.ethT[hi_ * 16:(hi_ + 1) * 16, :])
        inc(b, d, "d_eth", 16)
    wt(b, G, "d_eth", cnt(b, "d_eth"))
    for hi_ in range(2):
        d = G.dma_start(xgsl(b, XG_ETH + hi_ * 2048, 2048),
                        b.ethd[hi_:hi_ + 1, :].to_broadcast((128, 2048)))
        inc(b, d, "d_bce", 16)
    nbc = cnt(b, "d_bce")
    if _os.environ.get("KNOPV", "0") == "1":
        return

    wt(b, T, "aev", cnt(b, "aev"))
    for hi_, h in ((0, hA), (1, hB)):
        ethbc = xgsl(b, XG_ETH + hi_ * 2048, 2048)
        for c in range(NT):
            es = c % 2
            eap = xgsl(b, XG_E + es * 2048, 2048)
            emap = xgsl(b, XG_EM, 2048)
            gap = xgsl(b, XG_G + es * 2048, 2048)
            indap = xgsl(b, XG_IND, 2048)
            if c > 0 or hi_ > 0:
                wt(b, T, "p2e", cnt(b, "p2e"))
            for blk in range(NBLK):
                mm = T.matmul(b.psA[:, blk * 512:(blk + 1) * 512],
                              ksl(h, c * 128, 128), qsl(h, blk * 512, 512),
                              start=True, stop=True,
                              tile_position=((h % 2) * 64, 0))
            ntm = inc(b, mm, "tmm", 1)
            wt(b, A, "tmm", ntm)
            if c >= 2:
                wt(b, A, "p2g", cnt(b, "p2g") - 1)
            ev = A.activation(out=eap, in_=b.psA[:], func=AF.Exp, scale=0.125)
            nev = inc(b, ev, "p2e", 1)
            wt(b, V, "p2e", nev)
            if hi_ == 0 and c == 0:
                wt(b, V, "d_bce", nbc)
            V.tensor_scalar(out=emap, in0=eap, scalar1=-1.0, scalar2=None, op0=OP.add)
            p = V.tensor_tensor(out=indap, in0=eap, in1=ethbc, op=OP.is_ge)
            vg(b, p)
            g16 = V.tensor_tensor(out=gap, in0=emap, in1=indap, op=OP.mult)
            ngg = inc(b, g16, "p2g", 1)
            wt(b, T, "p2g", ngg)
            for blk in range(NBLK):
                mm = T.matmul(b.psB[0:65, blk * 512:(blk + 1) * 512],
                              b.vb[:, c * 1040 + h * 65:c * 1040 + h * 65 + 65],
                              gap[:, blk * 512:(blk + 1) * 512],
                              start=(c == 0), stop=(c == NT - 1))
            inc(b, mm, "tmm", 1)
        if _os.environ.get("KNOOC", "0") == "1":
            continue
        npv = cnt(b, "tmm")
        wt(b, V, "tmm", npv)
        wt(b, T, "p2e", cnt(b, "p2e"))
        for blk in range(NBLK):
            if blk > 0:
                wt(b, V, "tmm", cnt(b, "tmm"))
            zst = b.o1[0:1, 0:512]
            p = V.tensor_scalar(out=zst, in0=b.psB[64:65, blk * 512:(blk + 1) * 512],
                                scalar1=float(S), scalar2=None, op0=OP.add)
            vg(b, p)
            p = V.reciprocal(out=zst, in_=zst)
            vg(b, p)
            p = V.tensor_copy(b.rzb[0:1, 0:512], zst)
            vg(b, p)
            nvv = dinc(b, V, "vev", 1)
            wt(b, T, "vev", nvv)
            mm = T.matmul(b.psA[0:64, blk * 512:(blk + 1) * 512], b.onesr[0:1, 0:64],
                          b.rzb[0:1, 0:512], start=True, stop=True)
            ntm = inc(b, mm, "tmm", 1)
            wt(b, V, "tmm", ntm)
            p = V.tensor_scalar(out=b.o1[:], in0=b.psB[0:64, blk * 512:(blk + 1) * 512],
                                scalar1=b.vsums[0:64, h:h + 1], scalar2=None, op0=OP.add)
            vg(b, p)
            p0 = (h % 2) * 64
            onap = (b.xf[p0:p0 + 64,
                         (h // 2) * 1024 + blk * 256:(h // 2) * 1024 + blk * 256 + 256]
                    .bitcast(BF16))
            v = V.tensor_tensor(out=onap, in0=b.o1[:],
                                in1=b.psA[0:64, blk * 512:(blk + 1) * 512], op=OP.mult)
            inc(b, v, "p2e", 1)


def p5_router(b, bi):
    nc = b.nc
    T, V, A, G, Y = nc.tensor, nc.vector, nc.scalar, nc.gpsimd, nc.sync
    barrier(b)
    h2 = lambda c: b.qb[:, c * 2048:(c + 1) * 2048]
    aevposts = []
    for tt in range(NT):
        bank = tt % 4
        ps = b.psA[:, bank * 512:bank * 512 + 8]
        if tt >= 4:
            wt(b, T, "aev", aevposts[tt - 4])
        for kc in range(NCH):
            mm = T.matmul(ps, h2(kc)[:, tt * 128:(tt + 1) * 128], b.rwtb[:, kc * 8:(kc + 1) * 8],
                          start=(kc == 0), stop=False)
        mm = T.matmul(ps, b.onesr[0:1, 0:128], b.rbrow[0:1, :], start=False, stop=True)
        ntm = inc(b, mm, "tmm", 1)
        wt(b, V, "tmm", ntm)
        p = V.tensor_reduce(out=b.prsc[:, 0:1], in_=ps, axis=mybir.AxisListType.X, op=OP.max)
        vg(b, p)
        p = V.tensor_scalar(out=b.prsc[:, 1:2], in0=b.prsc[:, 0:1], scalar1=-1.0,
                            scalar2=None, op0=OP.mult)
        vg(b, p)
        nvv = dinc(b, V, "vev", 1)
        wt(b, A, "vev", nvv)
        ev = A.activation(out=b.prsc[:, 4:12], in_=ps, func=AF.Exp,
                          bias=b.prsc[:, 1:2], scale=1.0, accum_out=b.prsc[:, 2:3])
        nav = inc(b, ev, "aev", 1)
        aevposts.append(nav)
        wt(b, V, "aev", nav)
        p = V.reciprocal(out=b.prsc[:, 3:4], in_=b.prsc[:, 2:3])
        vg(b, p)
        p = V.tensor_scalar(out=b.prsc[:, 4:12], in0=b.prsc[:, 4:12], scalar1=b.prsc[:, 3:4],
                            scalar2=None, op0=OP.mult)
        vg(b, p)
        p = V.max(out=b.prsc[:, 12:20], in_=b.prsc[:, 4:12])
        vg(b, p)
        st = V.scalar_tensor_tensor(out=b.wall[:, tt * 8:(tt + 1) * 8], in0=b.prsc[:, 4:12],
                                    scalar=b.prsc[:, 13:14], in1=b.prsc[:, 4:12],
                                    op0=OP.is_ge, op1=OP.mult)
        inc(b, st, "vev", 2)
    if b.debug:
        wt(b, Y, "vev")
        dump(b, b.wall[:], b.dbg_w)


def p6_ffn(b, bi):
    nc = b.nc
    T, V, A, G, Y = nc.tensor, nc.vector, nc.scalar, nc.gpsimd, nc.sync
    barrier(b)
    h2 = lambda c: b.qb[:, c * 2048:(c + 1) * 2048]
    w1b = b.kb[:, 0:16384]
    w2b = b.vb[:, 0:16384]
    h1b = b.sr[:, 0:8192]
    mf = lambda tt: mixf(b, tt)

    for e in range(-1, NE):
        if e == -1:
            a1, b1s, a2, b2s = b.sw1, b.sb1.rearrange("(c p) -> p c", p=128), \
                b.sw2, b.sb2[None, :]
        else:
            a1, b1s, a2, b2s = b.rw1[e], b.rb1[e].rearrange("(c p) -> p c", p=128), \
                b.rw2[e], b.rb2[e][None, :]
        wt(b, G, "ff1", cnt(b, "ff1"))
        wt(b, G, "ffg", cnt(b, "ffg"))
        _d, _s = ld3(w1b, a1, FH)
        d = G.dma_start(_d, _s)
        inc(b, d, "d_w1", 16)
        d = G.dma_start(b.b1ch[:, 0:16], b1s, allow_slow_non_contiguous=True)
        ndw1 = inc(b, d, "d_w1", 16)
        wt(b, G, "ff2", cnt(b, "ff2"))
        wt(b, G, "ffm", cnt(b, "ffm"))
        _d, _s = ld3(w2b, a2, D)
        d = G.dma_start(_d, _s)
        inc(b, d, "d_w2", 16)
        d = G.dma_start(b.b2row[0:1, :], b2s)
        ndw = inc(b, d, "d_w2", 16)
        for tb in range(4):
            for oc in range(16):
                i = oc % 2
                ps = (b.psA if i == 0 else b.psB)[:, 0:512]
                if oc >= 2:
                    wt(b, T, "ffg", cnt(b, "ffg") - 1)
                wt(b, T, "d_w1", ndw1)
                last1 = None
                for kc in range(NCH):
                    mm = T.matmul(ps, w1b[:, kc * 2048 + oc * 128:kc * 2048 + oc * 128 + 128],
                                  h2(kc)[:, tb * 512:(tb + 1) * 512],
                                  start=(kc == 0), stop=(kc == NCH - 1))
                ntm = inc(b, mm, "tmm", 1)
                if tb == 3 and oc == 15:
                    dinc(b, T, "ff1", 1)
                wt(b, A, "tmm", ntm)
                ev = A.activation(out=h1b[:, oc * 512:(oc + 1) * 512], in_=ps, func=AF.Gelu,
                                  bias=b.b1ch[:, oc:oc + 1])
                inc(b, ev, "ffg", 1)
            ng = cnt(b, "ffg")
            wt(b, T, "ffg", ng)
            wt(b, T, "d_w2", ndw)
            for ts_ in range(4):
                tt = tb * 4 + ts_
                wcol = b.wall[:, tt * 8 + e:tt * 8 + e + 1] if e >= 0 else None
                for dh in range(2):
                    i = (ts_ * 2 + dh) % 2
                    ps = (b.psA if i == 0 else b.psB)[:, 512:1024]
                    if ts_ * 2 + dh >= 2:
                        wt(b, T, "ffm", cnt(b, "ffm") - 1)
                    for oc in range(16):
                        mm = T.matmul(ps, h1b[:, oc * 512 + ts_ * 128:oc * 512 + ts_ * 128 + 128],
                                      w2b[:, oc * 1024 + dh * 512:oc * 1024 + dh * 512 + 512],
                                      start=(oc == 0), stop=False)
                    mm = T.matmul(ps, b.onesr[0:1, 0:128],
                                  b.b2row[0:1, dh * 512:(dh + 1) * 512], start=False, stop=True)
                    ntm = inc(b, mm, "tmm", 1)
                    if tb == 3 and ts_ == 3 and dh == 1:
                        dinc(b, T, "ff2", 1)
                    wt(b, V, "tmm", ntm)
                    dst = mf(tt)[:, dh * 512:(dh + 1) * 512]
                    if e == -1:
                        v = V.tensor_scalar(out=dst, in0=ps, scalar1=1.0, scalar2=None,
                                            op0=OP.mult)
                    else:
                        v = V.scalar_tensor_tensor(out=dst, in0=ps, scalar=wcol, in1=dst,
                                                   op0=OP.mult, op1=OP.add)
                    inc(b, v, "ffm", 1)
        if b.debug and e in (-1, NE - 1):
            dbg = b.dbg_mx0 if e == -1 else b.dbg_mx1
            wt(b, Y, "ffm", cnt(b, "ffm"))
            for tt in range(NT):
                d = Y.dma_start(dbg[tt], mf(tt))
                inc(b, d, "d_out", 16)
            wt(b, Y, "d_out")


def p7_out(b, bi):
    nc = b.nc
    T, V, A, G, Y = nc.tensor, nc.vector, nc.scalar, nc.gpsimd, nc.sync
    barrier(b)
    d = Y.dma_start(gbbc(b), b.rowd[3:4, 0:1024].to_broadcast((128, 1024)))
    inc(b, d, "d_bc", 16)
    ldposts = []
    trposts = []
    vposts = []
    for tt in range(NT):
        sslot = (tt % 2) * 1024
        if tt >= 2:
            wt(b, Y, "tmm", trposts[tt - 2])
        x2sem = "d_x2a" if (tt % 2) == 0 else "d_x2b"
        for c in range(NCH):
            d = Y.dma_start(srf(b, sslot + c * 128, 128),
                            b.spillB[c][:, tt * 128:(tt + 1) * 128])
            inc(b, d, x2sem, 16)
        ldposts.append(cnt(b, x2sem))
        wt(b, T, x2sem, ldposts[tt])
        ps = b.psA if (tt % 2) == 0 else b.psB
        if tt >= 2:
            wt(b, T, "vev", vposts[tt - 2])
        for c in range(NCH):
            tr = T.transpose(ps[:, c * 128:(c + 1) * 128],
                             srf(b, sslot + c * 128, 128), b.identf[:])
            inc(b, tr, "tmm", 1)
        trposts.append(cnt(b, "tmm"))
        wt(b, V, "tmm", trposts[tt])
        if tt == 0:
            wt(b, V, "d_bc", cnt(b, "d_bc"))
        p = V.tensor_tensor(out=otsv(b), in0=mixf(b, tt), in1=gbbc(b),
                            op=OP.mult)
        vg(b, p)
        v = V.tensor_tensor(out=otsv(b), in0=otsv(b), in1=ps[:, 0:1024],
                            op=OP.add)
        vposts.append(inc(b, v, "vev", 1))
        wt(b, Y, "vev", vposts[tt])
        d = Y.dma_start(b.out[bi, tt * 128:(tt + 1) * 128, :], otsv(b))
        inc(b, d, "d_out", 16)


def kernel(**inputs):
    xs = np.ascontiguousarray(inputs["L0_attn_x"])
    cs = np.ascontiguousarray(inputs["L0_c"])
    names = {
        "wq": "L0_attn_Wq", "wk": "L0_attn_Wk", "wv": "L0_attn_Wv", "wo": "L0_attn_Wo",
        "bq": "L0_attn_bq", "bk": "L0_attn_bk", "bv": "L0_attn_bv", "bo": "L0_attn_bo",
        "rwt": "L0_ffn_router_w", "rbt": "L0_ffn_router_b",
        "sw1": "L0_ffn_sw1", "sb1": "L0_ffn_sb1", "sw2": "L0_ffn_sw2", "sb2": "L0_ffn_sb2",
        "rw1": "L0_ffn_rw1", "rb1": "L0_ffn_rb1", "rw2": "L0_ffn_rw2", "rb2": "L0_ffn_rb2",
        "wmod": "L0_wmod",
    }
    shared = {k: np.ascontiguousarray(np.asarray(inputs[v], dtype=np.float32))
              for k, v in names.items()}
    b = build(debug=False)
    emit(b)
    in_maps = []
    for c in range(8):
        m = dict(shared)
        m["x"] = np.ascontiguousarray(xs[2 * c:2 * c + 2])
        m["cc"] = np.ascontiguousarray(cs[2 * c:2 * c + 2])
        in_maps.append(m)
    res = run_bass_kernel_spmd(b.nc, in_maps, core_ids=list(range(8)))
    out = np.concatenate([r["out"] for r in res.results], axis=0).astype(np.float32)
    b.ctx.close()
    return out


# version 09 (final)
# speedup vs baseline: 1.1065x; 1.0826x over previous
import numpy as np
import concourse.bass as bass
import concourse.mybir as mybir
from concourse.bass_utils import run_bass_kernel_spmd
from concourse.alu_op_type import AluOpType as OP
from contextlib import ExitStack

F32 = mybir.dt.float32
BF16 = mybir.dt.bfloat16
I32 = mybir.dt.int32
AF = mybir.ActivationFunctionType

NB, S, D = 2, 2048, 1024
NH, DH, NPAIR = 16, 64, 8
KEEP = 204
FH, NE = 2048, 8
NCH, NT, NBLK = 8, 16, 4
T0, STEP0, NEG = 1.30, 0.45, -1.0e30
ZINIT, SSTEP = 1.25, 0.40
RPROBE = 4
DVE_ROUNDS = ()
SRING = 11
SRING_XG = (1, 2, 3, 5, 6, 7)
GW = 4

XG_E = 0
XG_G = 4096
XG_IND = 8192
XG_EM = 10240
XG_ETH = 12288


class B:
    pass


def inc(b, instr, name, n=1):
    s = b.sems[name]
    instr.then_inc(s[0], n)
    s[1] += n
    return s[1]


def cnt(b, name):
    return b.sems[name][1]


def dinc(b, eng, name, n=1):
    s = b.sems[name]
    eng.maybe_drain_then_inc((s[0], n))
    s[1] += n
    return s[1]


def wt(b, eng, name, val=None):
    if val is None:
        val = b.sems[name][1]
    if val > 0:
        eng.wait_ge(b.sems[name][0], val)


def vg(b, instr):
    n = inc(b, instr, "vraw", 1)
    b.nc.vector.wait_ge(b.sems["vraw"][0], n)


def barrier(b):
    nc = b.nc
    b.phase += 1
    pairs = [(nc.tensor, "bar_t"), (nc.vector, "bar_v"), (nc.scalar, "bar_a"),
             (nc.gpsimd, "bar_g"), (nc.sync, "bar_y")]
    for eng, nm in pairs:
        dinc(b, eng, nm, 1)
    for eng, nm in pairs:
        for _, nm2 in pairs:
            if nm2 != nm:
                eng.wait_ge(b.sems[nm2][0], b.phase)


def xslot(b, c):
    if c < 4:
        return b.xf[:, c * 2048:(c + 1) * 2048]
    return b.xg[:, (c - 4) * 4096:(c - 4) * 4096 + 4096].bitcast(F32)


def hslot(b, c):
    if c < 4:
        return b.xf[:, c * 2048:c * 2048 + 1024].bitcast(BF16)
    return b.xg[:, (c - 4) * 4096:(c - 4) * 4096 + 2048]


def xgsl(b, off, n):
    return b.xg[:, off:off + n]


def onorm(b, c):
    return b.xf[:, c * 1024:(c + 1) * 1024].bitcast(BF16)


def mixf(b, tt):
    if tt < 8:
        return b.xf[:, tt * 1024:(tt + 1) * 1024]
    return b.xg[:, (tt - 8) * 2048:(tt - 8) * 2048 + 2048].bitcast(F32)


def qbf(b, off, n):
    return b.qb[:, 2 * off:2 * off + 2 * n].bitcast(F32)


def kbf(b, off, n):
    return b.kb[:, 2 * off:2 * off + 2 * n].bitcast(F32)


def srf(b, off, n):
    return b.sr[:, 2 * off:2 * off + 2 * n].bitcast(F32)


def ld3(dst2d, srcdr, inner):
    return (dst2d.rearrange("p (c o) -> p c o", o=inner),
            srcdr.rearrange("(c p) o -> p c o", p=128))


def lnsc(b):
    return srf(b, 0, 2048)


def statsc(b, off, n):
    return b.sr[:, 4096 + off:4096 + off + n]


def wcolv(b, slot):
    return b.sr[:, 4096 + slot * 1024:4096 + slot * 1024 + 1024]


def gbbc(b):
    return srf(b, 2048, 1024)


def otsv(b, n=1024):
    return srf(b, 3072, n)


def build(debug=False):
    b = B()
    b.debug = debug
    nc = bass.Bass("TRN2", target_bir_lowering=False, detect_race_conditions=False)
    b.nc = nc
    ctx = ExitStack()
    b.ctx = ctx
    E = ctx.enter_context

    dri = lambda n, sh: nc.dram_tensor(n, sh, F32, kind="ExternalInput").ap()
    dro = lambda n, sh: nc.dram_tensor(n, sh, F32, kind="ExternalOutput").ap()

    def drt(n, sh):
        kind = "ExternalOutput" if debug else "Internal"
        return nc.dram_tensor(n, sh, F32, kind=kind).ap()

    b.x = dri("x", [NB, S, D])
    b.wq = dri("wq", [D, D]); b.wk = dri("wk", [D, D])
    b.wv = dri("wv", [D, D]); b.wo = dri("wo", [D, D])
    b.bq = dri("bq", [D]); b.bk = dri("bk", [D])
    b.bv = dri("bv", [D]); b.bo = dri("bo", [D])
    b.rwt = dri("rwt", [D, NE]); b.rbt = dri("rbt", [NE])
    b.sw1 = dri("sw1", [D, FH]); b.sb1 = dri("sb1", [FH])
    b.sw2 = dri("sw2", [FH, D]); b.sb2 = dri("sb2", [D])
    b.rw1 = dri("rw1", [NE, D, FH]); b.rb1 = dri("rb1", [NE, FH])
    b.rw2 = dri("rw2", [NE, FH, D]); b.rb2 = dri("rb2", [NE, D])
    b.cc = dri("cc", [NB, D]); b.wmod = dri("wmod", [D, 6 * D])
    b.out = dro("out", [NB, S, D])

    b.spillA = drt("spillA", [NCH, 128, S])
    b.spillB = drt("spillB", [NCH, 128, S])
    b.ethd = nc.dram_tensor("ethd", [2, S], F32, kind="Internal").ap()
    b.rowd = nc.dram_tensor("rowd", [4, S], F32, kind="Internal").ap()
    if debug:
        b.dbg_h = dro("dbg_h", [NCH, 128, S])
        b.dbg_h2 = dro("dbg_h2", [NCH, 128, S])
        b.dbg_ts = dro("dbg_ts", [NB, NPAIR, 128, 32])
        b.dbg_mod = dro("dbg_mod", [NB, 128, 48])
        b.dbg_w = dro("dbg_w", [128, 128])
        b.dbg_mx0 = dro("dbg_mx0", [NT, 128, 1024])
        b.dbg_mx1 = dro("dbg_mx1", [NT, 128, 1024])
        b.dbg_q = dro("dbg_q", [128, S])
        b.dbg_k = dro("dbg_k", [128, S])
        b.dbg_v = dro("dbg_v", [128, 2080])

    b.xf = E(nc.sbuf_tensor("xf", [128, 8192], F32))
    b.xg = E(nc.sbuf_tensor("xg", [128, 16384], BF16))
    b.rwr = E(nc.sbuf_tensor("rwr", [1, 2048], F32))
    b.qb = E(nc.sbuf_tensor("qb", [128, 16384], BF16))
    b.kb = E(nc.sbuf_tensor("kb", [128, 16384], BF16))
    b.vb = E(nc.sbuf_tensor("vb", [128, 16640], BF16))
    b.sr = E(nc.sbuf_tensor("sr", [128, 11264], BF16))
    b.identf = E(nc.sbuf_tensor("identf", [128, 128], F32))
    b.io32 = E(nc.sbuf_tensor("io32", [128, 128], I32))
    b.onesr = E(nc.sbuf_tensor("onesr", [1, 512], BF16))
    b.onesc = E(nc.sbuf_tensor("onesc", [128, 1], BF16))
    b.modv = E(nc.sbuf_tensor("modv", [128, 48], F32))
    b.csil = E(nc.sbuf_tensor("csil", [128, 8], BF16))
    b.craw = E(nc.sbuf_tensor("craw", [128, 8], F32))
    b.bqs = E(nc.sbuf_tensor("bqs", [128, 16], F32))
    b.bvrow = E(nc.sbuf_tensor("bvrow", [1, 1024], BF16))
    b.borow = E(nc.sbuf_tensor("borow", [1, 1024], BF16))
    b.b2row = E(nc.sbuf_tensor("b2row", [2, 1024], BF16))
    b.rbrow = E(nc.sbuf_tensor("rbrow", [1, 8], BF16))
    b.b1ch = E(nc.sbuf_tensor("b1ch", [128, 32], F32))
    b.rwtb = E(nc.sbuf_tensor("rwtb", [128, 64], BF16))
    for nm in ("lo", "hi", "chi", "clo", "pr", "cn", "ts", "s1", "s2", "s3",
               "sm", "sq", "sd", "u1", "u2"):
        setattr(b, "st_" + nm, E(nc.sbuf_tensor("st_" + nm, [128, 32], F32)))
    b.w9 = E(nc.sbuf_tensor("w9", [128, 448], F32))
    b.iota9 = E(nc.sbuf_tensor("iota9", [128, 9], F32))
    b.vsums = E(nc.sbuf_tensor("vsums", [65, 16], F32))
    b.eth = E(nc.sbuf_tensor("eth", [128, 32], F32))
    b.ethT = E(nc.sbuf_tensor("ethT", [32, 128], F32))
    b.rzb = E(nc.sbuf_tensor("rzb", [1, 512], BF16))
    b.gbt = E(nc.sbuf_tensor("gbt", [8, 128], F32))
    b.o1 = E(nc.sbuf_tensor("o1", [64, 512], F32))
    b.wall = E(nc.sbuf_tensor("wall", [128, 128], F32))
    b.prsc = E(nc.sbuf_tensor("prsc", [128, 24], F32))
    b.cst = E(nc.sbuf_tensor("cst", [128, 4], F32))

    b.psA = E(nc.psum_tensor("psA", [128, 2048], F32))
    b.psB = E(nc.psum_tensor("psB", [128, 2048], F32))

    b.sems = {}
    for name in ("bar_t", "bar_v", "bar_a", "bar_g", "bar_y",
                 "d_x", "d_xa", "d_xb", "d_bq", "d_w", "d_wma", "d_wmb", "d_w1", "d_w2",
                 "d_sp", "d_eth", "d_out", "d_row", "d_bc", "d_bce", "d_x2", "d_x2a", "d_x2b",
                 "tmm", "aev", "vev", "sfree", "upd", "prd", "pra",
                 "p2e", "p2g", "resid", "ethx",
                 "ff1", "ff2", "ffg", "ffm", "aux1", "aux2", "aux3", "aux4", "vraw"):
        b.sems[name] = [E(nc.semaphore(name)), 0]
    b.phase = 0
    b.guctr = 0
    b.sfree_post = {}
    b.block = E(nc.Block())
    return b


def emit(b):
    nc = b.nc
    T, V, A, G, Y = nc.tensor, nc.vector, nc.scalar, nc.gpsimd, nc.sync

    d = G.iota(b.io32[:], pattern=[[1, 128]], base=0, channel_multiplier=-1)
    inc(b, d, "aux1", 1)
    iv = b.w9[:, 0:9].bitcast(I32)
    d = G.iota(iv, pattern=[[1, 9]], base=0, channel_multiplier=0)
    inc(b, d, "aux1", 1)
    wt(b, V, "aux1", 2)
    V.tensor_scalar(out=b.identf[:], in0=b.io32[:], scalar1=0.0, scalar2=None, op0=OP.is_equal)
    V.tensor_copy(b.iota9[:], iv)
    V.memset(b.onesr[:], 1.0)
    V.memset(b.onesc[:], 1.0)
    V.memset(b.cst[:, 0:1], 1e-5)
    dinc(b, V, "aux2", 1)

    d = G.dma_start(b.bvrow[:], b.bv[None, :]); inc(b, d, "d_w", 16)
    d = G.dma_start(b.borow[:], b.bo[None, :]); inc(b, d, "d_w", 16)
    d = G.dma_start(b.rbrow[:], b.rbt[None, :]); inc(b, d, "d_w", 16)
    _d, _s = ld3(b.rwtb[:], b.rwt, NE)
    d = G.dma_start(_d, _s); inc(b, d, "d_w", 16)
    d = Y.dma_start(b.bqs[:, 0:8], b.bq.rearrange("(c p) -> p c", p=128), allow_slow_non_contiguous=True); inc(b, d, "d_bq", 16)
    d = Y.dma_start(b.bqs[:, 8:16], b.bk.rearrange("(c p) -> p c", p=128), allow_slow_non_contiguous=True); inc(b, d, "d_bq", 16)

    import os
    stage = int(os.environ.get("KSTAGE", "8"))
    phases = [p0_load_mod, p1_ln1, p2_qkv, p3_attention, p4_ln2, p5_router, p6_ffn, p7_out]
    for bi in range(NB):
        for pi, ph in enumerate(phases[:stage]):
            ph(b, bi)

    wt(b, Y, "d_out")


def dump(b, src_f32_ap, dst_dram_ap):
    nc = b.nc
    V, Y = nc.vector, nc.sync
    d = Y.dma_start(dst_dram_ap, src_f32_ap)
    n = inc(b, d, "d_out", 16)
    wt(b, Y, "d_out", n)


def p0_load_mod(b, bi):
    nc = b.nc
    T, V, A, G, Y = nc.tensor, nc.vector, nc.scalar, nc.gpsimd, nc.sync
    barrier(b)

    d = Y.dma_start(b.craw[:], b.cc[bi].rearrange("(c p) -> p c", p=128), allow_slow_non_contiguous=True)
    inc(b, d, "d_row", 16)
    wt(b, A, "d_row")
    A.activation(out=b.craw[:], in_=b.craw[:], func=AF.Silu)
    dinc(b, A, "aux3", 1)
    wt(b, V, "aux3")
    V.tensor_copy(b.csil[:], b.craw[:])
    dinc(b, V, "aux3", 1)

    wt(b, T, "aux2", 1)
    tposted = []
    for t in range(NT):
        buf = srf(b, (t % 2) * 1024, 1024)
        if t >= 2:
            wt(b, Y, "tmm", tposted[t - 2])
        xsem = "d_xa" if (t % 2) == 0 else "d_xb"
        d = Y.dma_start(buf, b.x[bi, t * 128:(t + 1) * 128, :])
        nld = inc(b, d, xsem, 16)
        wt(b, T, xsem, nld)
        if t >= 2:
            wt(b, T, "aev", cnt(b, "aev") - NCH)
        ps = b.psA if (t % 2) == 0 else b.psB
        for c in range(NCH):
            tr = T.transpose(ps[:, c * 128:(c + 1) * 128], buf[:, c * 128:(c + 1) * 128],
                             b.identf[:])
            inc(b, tr, "tmm", 1)
        tposted.append(cnt(b, "tmm"))
        wt(b, A, "tmm")
        wt(b, V, "tmm")
        for c in range(NCH):
            dstx = xslot(b, c)[:, t * 128:(t + 1) * 128]
            if c < 4:
                ev = A.activation(out=dstx, in_=ps[:, c * 128:(c + 1) * 128], func=AF.Copy)
            else:
                ev = V.tensor_copy(dstx, ps[:, c * 128:(c + 1) * 128])
            inc(b, ev, "aev", 1)

    wt(b, T, "aev")
    wt(b, T, "aux3", cnt(b, "aux3"))
    wcons = []
    for oc in range(48):
        w = wcolv(b, oc % 2)
        if oc >= 2:
            wt(b, G, "tmm", wcons[oc - 2])
        _d, _s = ld3(w, b.wmod[:, oc * 128:(oc + 1) * 128], 128)
        d = G.dma_start(_d, _s)
        wsem = "d_wma" if (oc % 2) == 0 else "d_wmb"
        ndw = inc(b, d, wsem, 16)
        wt(b, T, wsem, ndw)
        for kc in range(NCH):
            mm = T.matmul(b.psA[:, 1024 + oc:1024 + oc + 1], w[:, kc * 128:(kc + 1) * 128],
                          b.csil[:, kc:kc + 1], start=(kc == 0), stop=(kc == NCH - 1),
                          skip_group_check=True)
        inc(b, mm, "tmm", 1)
        wcons.append(cnt(b, "tmm"))
    wt(b, A, "tmm")
    ev = A.activation(out=b.modv[:], in_=b.psA[:, 1024:1072], func=AF.Copy)
    inc(b, ev, "aev", 1)
    wt(b, V, "aev")
    V.tensor_scalar(out=b.modv[:, 8:16], in0=b.modv[:, 8:16], scalar1=1.0, scalar2=None, op0=OP.add)
    V.tensor_scalar(out=b.modv[:, 32:40], in0=b.modv[:, 32:40], scalar1=1.0, scalar2=None, op0=OP.add)
    for col in (16, 40):
        sl = b.modv[:, col:col + 8]
        p = V.tensor_scalar(out=b.st_s1[:, 0:8], in0=sl, scalar1=0.0, scalar2=2.0,
                            op0=OP.is_ge, op1=OP.mult)
        vg(b, p)
        p = V.tensor_scalar(out=b.st_s1[:, 0:8], in0=b.st_s1[:, 0:8], scalar1=-1.0,
                            scalar2=None, op0=OP.add)
        p = V.tensor_scalar(out=b.st_s2[:, 0:8], in0=sl, scalar1=-1.0, scalar2=None,
                            op0=OP.mult)
        vg(b, p)
        p = V.tensor_tensor(out=b.st_s2[:, 0:8], in0=sl, in1=b.st_s2[:, 0:8], op=OP.max)
        vg(b, p)
        p = V.tensor_scalar(out=b.st_s2[:, 0:8], in0=b.st_s2[:, 0:8], scalar1=0.25,
                            scalar2=None, op0=OP.max)
        vg(b, p)
        p = V.tensor_tensor(out=sl, in0=b.st_s1[:, 0:8], in1=b.st_s2[:, 0:8], op=OP.mult)
        vg(b, p)
    dinc(b, V, "aux4", 1)

    wt(b, T, "aux4", cnt(b, "aux4"))
    tr = T.transpose(b.psB[0:8, 0:128], b.modv[:, 40:48], b.identf[:])
    ntm = inc(b, tr, "tmm", 1)
    wt(b, V, "tmm", ntm)
    V.tensor_copy(b.gbt[:], b.psB[0:8, 0:128])
    nvv = dinc(b, V, "vev", 1)
    wt(b, Y, "vev", nvv)
    d = Y.dma_start(b.rowd[3:4, 0:1024].rearrange("a (c q) -> (a c) q", q=128), b.gbt[:])
    ndr = inc(b, d, "d_row", 16)
    wt(b, Y, "d_row", ndr)
    if b.debug:
        dump(b, b.modv[:], b.dbg_mod[bi, :, :])


def ln_phase(b, bi, src_resident, bchome, ca_col, sa_col, dst_xb, spill):
    nc = b.nc
    T, V, A, G, Y = nc.tensor, nc.vector, nc.scalar, nc.gpsimd, nc.sync
    home = qbf if bchome == "qb" else kbf
    stream = not src_resident
    vdone = []

    for c in range(NCH):
        if stream:
            xcb = b.xf[:, (c % 2) * 2048:(c % 2) * 2048 + 2048]
            x2sem = "d_x2a" if (c % 2) == 0 else "d_x2b"
            if c >= 2:
                wt(b, Y, "vev", vdone[c - 2])
            d = Y.dma_start(xcb, b.spillB[c])
            nld = inc(b, d, x2sem, 16)
            wt(b, V, x2sem, nld)
        else:
            xcb = xslot(b, c)
        V.tensor_copy(statsc(b, 0, 2048), xcb)
        V.tensor_tensor(out=statsc(b, 2048, 2048), in0=xcb, in1=xcb, op=OP.mult)
        nvv = dinc(b, V, "vev", 1)
        vdone.append(nvv)
        wt(b, T, "vev", nvv)
        for blk in range(NBLK):
            mm = T.matmul(b.psA[0:1, blk * 512:(blk + 1) * 512], b.onesc[:],
                          statsc(b, blk * 512, 512),
                          start=(c == 0), stop=(c == NCH - 1), skip_group_check=True)
            mm = T.matmul(b.psB[0:1, blk * 512:(blk + 1) * 512], b.onesc[:],
                          statsc(b, 2048 + blk * 512, 512),
                          start=(c == 0), stop=(c == NCH - 1), skip_group_check=True)
        inc(b, mm, "tmm", 1)

    wt(b, V, "tmm")
    L1 = lnsc(b)[0:1, :]
    L2 = srf(b, 3072, 2048)[0:1, :]
    V.tensor_scalar(out=L1, in0=b.psA[0:1, :], scalar1=1.0 / D, scalar2=None, op0=OP.mult)
    V.tensor_scalar(out=L2, in0=b.psB[0:1, :], scalar1=1.0 / D, scalar2=None, op0=OP.mult)
    nvv = dinc(b, V, "vev", 1)
    wt(b, Y, "vev", nvv)
    d = Y.dma_start(b.rowd[0:1, :], L1)
    ndr = inc(b, d, "d_row", 16)
    wt(b, V, "d_row", ndr)
    V.tensor_tensor(out=b.rwr[0:1, :], in0=L1, in1=L1, op=OP.mult)
    V.tensor_tensor(out=b.rwr[0:1, :], in0=L2, in1=b.rwr[0:1, :], op=OP.subtract)
    nvv = dinc(b, V, "vev", 1)
    wt(b, A, "vev", nvv)
    A.activation(out=b.rwr[0:1, :], in_=b.rwr[0:1, :], func=AF.Sqrt,
                 bias=b.cst[0:1, 0:1], scale=1.0)
    nav = dinc(b, A, "aev", 1)
    wt(b, V, "aev", nav)
    V.reciprocal(out=b.rwr[0:1, :], in_=b.rwr[0:1, :])
    nvv = dinc(b, V, "vev", 1)
    wt(b, Y, "vev", nvv)
    d = Y.dma_start(b.rowd[1:2, :], b.rwr[0:1, :])
    ndr = inc(b, d, "d_row", 16)
    wt(b, Y, "d_row", ndr)
    d = Y.dma_start(home(b, 0, 2048), b.rowd[0:1, :].to_broadcast((128, 2048)))
    inc(b, d, "d_bc", 16)
    d = Y.dma_start(home(b, 2048, 2048), b.rowd[1:2, :].to_broadcast((128, 2048)))
    nbc = inc(b, d, "d_bc", 16)
    wt(b, V, "d_bc", nbc)
    wt(b, V, "aux4", cnt(b, "aux4"))

    for c in range(NCH):
        if stream:
            xcb = b.xf[:, (c % 2) * 2048:(c % 2) * 2048 + 2048]
            x2sem = "d_x2a" if (c % 2) == 0 else "d_x2b"
            wt(b, Y, "vev", vdone[NCH + c - 2] if c >= 2 else vdone[NCH - 2 + c])
            d = Y.dma_start(xcb, b.spillB[c])
            nld = inc(b, d, x2sem, 16)
            wt(b, V, x2sem, nld)
        else:
            xcb = xslot(b, c)
            if c >= 1:
                wt(b, Y, "vev", vdone[-1])
            d = Y.dma_start(spill[c], xcb)
            nsp = inc(b, d, "d_sp", 16)
        V.tensor_tensor(out=lnsc(b), in0=xcb, in1=home(b, 0, 2048), op=OP.subtract)
        V.tensor_tensor(out=lnsc(b), in0=lnsc(b), in1=home(b, 2048, 2048), op=OP.mult)
        if not stream:
            wt(b, V, "d_sp", nsp)
        dstap = (hslot(b, c) if dst_xb else b.qb[:, c * 2048:(c + 1) * 2048])
        V.tensor_scalar(out=dstap, in0=lnsc(b),
                        scalar1=b.modv[:, ca_col + c:ca_col + c + 1],
                        scalar2=b.modv[:, sa_col + c:sa_col + c + 1],
                        op0=OP.mult, op1=OP.add)
        nvv = dinc(b, V, "vev", 1)
        vdone.append(nvv)


def p1_ln1(b, bi):
    barrier(b)
    ln_phase(b, bi, src_resident=True, bchome="qb", ca_col=8, sa_col=0, dst_xb=True,
             spill=b.spillA)
    if b.debug:
        nc = b.nc
        V, Y = nc.vector, nc.sync
        for c in range(NCH):
            wt(b, V, "vev")
            V.tensor_copy(lnsc(b), hslot(b, c))
            n = dinc(b, V, "vev", 1)
            wt(b, Y, "vev", n)
            dump(b, lnsc(b), b.dbg_h[c])
            wt(b, V, "d_out")


def p4_ln2(b, bi):
    barrier(b)
    ln_phase(b, bi, src_resident=False, bchome="kb", ca_col=32, sa_col=24, dst_xb=False,
             spill=None)
    if b.debug:
        nc = b.nc
        V, Y = nc.vector, nc.sync
        for c in range(NCH):
            wt(b, V, "vev")
            V.tensor_copy(lnsc(b), b.qb[:, c * 2048:(c + 1) * 2048])
            n = dinc(b, V, "vev", 1)
            wt(b, Y, "vev", n)
            dump(b, lnsc(b), b.dbg_h2[c])
            wt(b, V, "d_out")


def p2_qkv(b, bi):
    nc = b.nc
    T, V, A, G, Y = nc.tensor, nc.vector, nc.scalar, nc.gpsimd, nc.sync
    barrier(b)
    wb = b.sr[:, 0:8192]
    h = lambda c: hslot(b, c)

    wt(b, A, "d_bq", 32)
    for wi, (wdr, dst, bcol) in enumerate([(b.wq, b.qb, 0), (b.wk, b.kb, 8)]):
        if wi > 0:
            wt(b, G, "tmm", cnt(b, "tmm"))
        _d, _s = ld3(wb, wdr, 1024)
        d = G.dma_start(_d, _s)
        ndw = inc(b, d, "d_w", 16)
        wt(b, T, "d_w", ndw)
        for i in range(NCH * NBLK):
            oc, blk = divmod(i, NBLK)
            pss = b.psA if (i % 2) == 0 else b.psB
            bank = (i // 2) % 4
            ps = pss[:, bank * 512:(bank + 1) * 512]
            if i >= 8:
                wt(b, T, "aev", cnt(b, "aev") - 7)
            for kc in range(NCH):
                mm = T.matmul(ps, wb[:, kc * 1024 + oc * 128:kc * 1024 + oc * 128 + 128],
                              h(kc)[:, blk * 512:(blk + 1) * 512],
                              start=(kc == 0), stop=(kc == NCH - 1))
            ntm = inc(b, mm, "tmm", 1)
            wt(b, A, "tmm", ntm)
            ev = A.activation(out=dst[:, oc * 2048 + blk * 512:oc * 2048 + blk * 512 + 512],
                              in_=ps, func=AF.Identity, bias=b.bqs[:, bcol + oc:bcol + oc + 1])
            inc(b, ev, "aev", 1)

    wt(b, G, "tmm", cnt(b, "tmm"))
    _d, _s = ld3(wb, b.wv, 1024)
    d = G.dma_start(_d, _s)
    ndw = inc(b, d, "d_w", 16)
    wt(b, V, "aev", cnt(b, "aev"))
    vv = b.vb[:].rearrange("p (t hh j) -> p t hh j", hh=16, j=65)
    V.memset(vv[:, :, :, 64:65], 1.0)
    dinc(b, V, "vev", 1)
    wt(b, T, "d_w", ndw)
    wt(b, A, "vev", cnt(b, "vev"))
    for i in range(NT * 2):
        tt, half = divmod(i, 2)
        pss = b.psA if (i % 2) == 0 else b.psB
        bank = (i // 2) % 4
        ps = pss[:, bank * 512:(bank + 1) * 512]
        if i >= 8:
            wt(b, T, "aev", cnt(b, "aev") - 7)
        for kc in range(NCH):
            mm = T.matmul(ps, h(kc)[:, tt * 128:(tt + 1) * 128],
                          wb[:, kc * 1024 + half * 512:kc * 1024 + half * 512 + 512],
                          start=(kc == 0), stop=False)
        mm = T.matmul(ps, b.onesr[0:1, 0:128],
                      b.bvrow[0:1, half * 512:(half + 1) * 512], start=False, stop=True)
        ntm = inc(b, mm, "tmm", 1)
        wt(b, A, "tmm", ntm)
        dstv = (b.vb[:, tt * 1040 + half * 520:tt * 1040 + half * 520 + 520]
                .rearrange("p (hh j) -> p hh j", j=65)[:, :, 0:64])
        ev = A.activation(out=dstv, in_=ps.rearrange("p (hh j) -> p hh j", j=64), func=AF.Copy)
        inc(b, ev, "aev", 1)

    if b.debug:
        wt(b, V, "aev")
        V.tensor_copy(otsv(b, 2048), b.qb[:, 0:2048])
        n = dinc(b, V, "vev", 1)
        wt(b, Y, "vev", n); dump(b, otsv(b, 2048), b.dbg_q); wt(b, V, "d_out")
        V.tensor_copy(otsv(b, 2048), b.kb[:, 0:2048])
        n = dinc(b, V, "vev", 1)
        wt(b, Y, "vev", n); dump(b, otsv(b, 2048), b.dbg_k); wt(b, V, "d_out")
        V.tensor_copy(otsv(b, 2048), b.vb[:, 0:2048])
        n = dinc(b, V, "vev", 1)
        wt(b, Y, "vev", n); dump(b, otsv(b, 2048), b.dbg_v[:, 0:2048]); wt(b, V, "d_out")


def p3_attention(b, bi):
    nc = b.nc
    T, V, A, G, Y = nc.tensor, nc.vector, nc.scalar, nc.gpsimd, nc.sync
    barrier(b)

    wt(b, T, "aux2", 1)
    for hh in range(NH):
        for c in range(NT):
            mm = T.matmul(b.psA[0:65, hh:hh + 1],
                          b.vb[:, c * 1040 + hh * 65:c * 1040 + hh * 65 + 65],
                          b.onesc[:], start=(c == 0), stop=(c == NT - 1),
                          skip_group_check=True)
    ntm = inc(b, mm, "tmm", 1)
    wt(b, A, "tmm", ntm)
    ev = A.activation(out=b.vsums[:], in_=b.psA[0:65, 0:16], func=AF.Copy)
    nav = inc(b, ev, "aev", 1)
    wt(b, T, "aev", nav)

    import os as _os
    for pair in range(int(_os.environ.get("KPAIRS", str(NPAIR)))):
        attn_pair(b, bi, pair)

    wb = b.vb[:, 0:8192]
    wt(b, G, "sfree", cnt(b, "sfree"))
    _d, _s = ld3(wb, b.wo, 1024)
    d = G.dma_start(_d, _s)
    ndw = inc(b, d, "d_w", 16)
    wt(b, T, "d_w", ndw)
    on = lambda c: onorm(b, c)
    wt(b, Y, "sfree", cnt(b, "sfree"))
    for oc in range(NCH):
        xcb = srf(b, (oc % 2) * 2048, 2048)
        x2sem = "d_x2a" if (oc % 2) == 0 else "d_x2b"
        if oc >= 2:
            wt(b, Y, "resid", cnt(b, "resid") - NBLK)
            wt(b, Y, "d_sp", cnt(b, "d_sp"))
        d = Y.dma_start(xcb, b.spillA[oc])
        nld = inc(b, d, x2sem, 16)
        for blk in range(NBLK):
            i = oc * NBLK + blk
            pss = b.psA if (i % 2) == 0 else b.psB
            bank = (i // 2) % 4
            ps = pss[:, bank * 512:(bank + 1) * 512]
            if i >= 8:
                wt(b, T, "resid", cnt(b, "resid") - 7)
            for kc in range(NCH):
                mm = T.matmul(ps, wb[:, kc * 1024 + oc * 128:kc * 1024 + oc * 128 + 128],
                              on(kc)[:, blk * 512:(blk + 1) * 512],
                              start=(kc == 0), stop=False)
            mm = T.matmul(ps, b.borow[0:1, oc * 128:(oc + 1) * 128],
                          b.onesr[0:1, 0:512], start=False, stop=True)
            ntm = inc(b, mm, "tmm", 1)
            wt(b, V, "tmm", ntm)
            wt(b, V, x2sem, nld)
            st = V.scalar_tensor_tensor(out=xcb[:, blk * 512:(blk + 1) * 512],
                                        in0=ps, scalar=b.modv[:, 16 + oc:16 + oc + 1],
                                        in1=xcb[:, blk * 512:(blk + 1) * 512],
                                        op0=OP.mult, op1=OP.add)
            inc(b, st, "resid", 1)
        wt(b, Y, "resid", cnt(b, "resid"))
        d = Y.dma_start(b.spillB[oc], xcb)
        inc(b, d, "d_sp", 16)
    wt(b, Y, "d_sp")


def attn_pair(b, bi, pair):
    nc = b.nc
    T, V, A, G, Y = nc.tensor, nc.vector, nc.scalar, nc.gpsimd, nc.sync
    hA, hB = 2 * pair, 2 * pair + 1

    def qsl(h, c0, n):
        p0 = (h % 2) * 64
        return b.qb[p0:p0 + 64, (h // 2) * 2048 + c0:(h // 2) * 2048 + c0 + n]

    def ksl(h, c0, n):
        p0 = (h % 2) * 64
        return b.kb[p0:p0 + 64, (h // 2) * 2048 + c0:(h // 2) * 2048 + c0 + n]

    def s16ap(u):
        s = (b.guctr + u) % SRING
        if s < 5:
            return b.sr[:, s * 2048:(s + 1) * 2048]
        return xgsl(b, SRING_XG[s - 5] * 2048, 2048)

    wt(b, V, "ethx", cnt(b, "ethx"))
    V.memset(b.st_lo[:], NEG)
    V.memset(b.st_hi[:], 1.0e30)
    V.memset(b.st_chi[:], 0.0)
    V.memset(b.st_clo[:], float(S))
    p = V.memset(b.st_pr[:], T0)
    vg(b, p)
    dinc(b, V, "upd", 1)

    wt(b, T, "p2e", cnt(b, "p2e"))
    evposts = {}
    def emit_probes(units, r, upd_need):
        dve = r in DVE_ROUNDS
        for u in units:
            if dve:
                wt(b, V, "aev", evposts[u])
                wt(b, V, "upd", upd_need)
                pr = V.tensor_scalar(out=xgsl(b, XG_IND, 2048),
                                     in0=s16ap(u), scalar1=b.st_pr[:, u:u + 1],
                                     scalar2=0.0, op0=OP.is_ge, op1=OP.add,
                                     accum_out=b.st_cn[:, u:u + 1])
                inc(b, pr, "prd", 1)
            else:
                wt(b, A, "aev", evposts[u])
                wt(b, A, "upd", upd_need)
                pr = A.activation(out=xgsl(b, XG_E, 2048),
                                  in_=s16ap(u), func=AF.Sign,
                                  bias=b.st_pr[:, u:u + 1], scale=-1.0,
                                  accum_out=b.st_cn[:, u:u + 1])
                inc(b, pr, "pra", 1)
        return cnt(b, "prd" if dve else "pra")

    def emit_update(units, r, npr):
        dve = r in DVE_ROUNDS
        step_r = STEP0 * (2.0 ** r)
        u0, n = units[0], len(units)
        L = lambda t: t[:, u0:u0 + n]
        if dve:
            wt(b, V, "prd", npr)
        else:
            wt(b, V, "pra", npr)
            p = V.tensor_scalar(out=L(b.st_cn), in0=L(b.st_cn), scalar1=-0.5,
                                scalar2=1024.0, op0=OP.mult, op1=OP.add)
            vg(b, p)
        p = V.tensor_scalar(out=L(b.st_s1), in0=L(b.st_cn), scalar1=float(KEEP),
                            scalar2=None, op0=OP.is_ge)
        vg(b, p)
        m1 = L(b.st_s1).bitcast(I32)
        V.copy_predicated(out=L(b.st_lo), mask=m1, data=L(b.st_pr))
        V.copy_predicated(out=L(b.st_clo), mask=m1, data=L(b.st_cn))
        p = V.tensor_scalar(out=L(b.st_s2), in0=L(b.st_s1), scalar1=-1.0, scalar2=1.0,
                            op0=OP.mult, op1=OP.add)
        vg(b, p)
        m2 = L(b.st_s2).bitcast(I32)
        V.copy_predicated(out=L(b.st_hi), mask=m2, data=L(b.st_pr))
        p = V.copy_predicated(out=L(b.st_chi), mask=m2, data=L(b.st_cn))
        if r < RPROBE - 1:
            vg(b, p)
            V.tensor_scalar(out=L(b.st_s1), in0=L(b.st_lo), scalar1=-1.0e29, scalar2=None,
                            op0=OP.is_le)
            V.tensor_scalar(out=L(b.st_s2), in0=L(b.st_hi), scalar1=1.0e29, scalar2=None,
                            op0=OP.is_ge)
            p = V.tensor_tensor(out=L(b.st_u1), in0=L(b.st_hi), in1=L(b.st_lo),
                                op=OP.subtract)
            p = V.tensor_tensor(out=L(b.st_u2), in0=L(b.st_clo), in1=L(b.st_chi),
                                op=OP.subtract)
            vg(b, p)
            p = V.reciprocal(out=L(b.st_u2), in_=L(b.st_u2))
            p2 = V.tensor_scalar(out=L(b.st_cn), in0=L(b.st_cn), scalar1=-float(KEEP),
                                 scalar2=None, op0=OP.add)
            vg(b, p2)
            p = V.tensor_tensor(out=L(b.st_u2), in0=L(b.st_u2), in1=L(b.st_cn), op=OP.mult)
            vg(b, p)
            p = V.tensor_tensor(out=L(b.st_u2), in0=L(b.st_u2), in1=L(b.st_u1), op=OP.mult)
            vg(b, p)
            p = V.tensor_tensor(out=L(b.st_u2), in0=L(b.st_pr), in1=L(b.st_u2), op=OP.add)
            p2 = V.tensor_scalar(out=L(b.st_u1), in0=L(b.st_u1), scalar1=0.05, scalar2=None,
                                 op0=OP.mult)
            vg(b, p2)
            p = V.tensor_tensor(out=L(b.st_s3), in0=L(b.st_lo), in1=L(b.st_u1), op=OP.add)
            vg(b, p)
            p = V.tensor_tensor(out=L(b.st_u2), in0=L(b.st_u2), in1=L(b.st_s3), op=OP.max)
            p2 = V.tensor_tensor(out=L(b.st_s3), in0=L(b.st_hi), in1=L(b.st_u1),
                                 op=OP.subtract)
            vg(b, p2)
            p = V.tensor_tensor(out=L(b.st_pr), in0=L(b.st_u2), in1=L(b.st_s3), op=OP.min)
            p2 = V.tensor_scalar(out=L(b.st_u1), in0=L(b.st_sd),
                                 scalar1=float(SSTEP * (2.0 ** r)), scalar2=None, op0=OP.mult)
            vg(b, p2)
            p = V.tensor_tensor(out=L(b.st_s3), in0=L(b.st_hi), in1=L(b.st_u1),
                                op=OP.subtract)
            vg(b, p)
            V.copy_predicated(out=L(b.st_pr), mask=L(b.st_s1).bitcast(I32), data=L(b.st_s3))
            p = V.tensor_tensor(out=L(b.st_s3), in0=L(b.st_lo), in1=L(b.st_u1), op=OP.add)
            vg(b, p)
            p = V.copy_predicated(out=L(b.st_pr), mask=L(b.st_s2).bitcast(I32),
                                  data=L(b.st_s3))
            vg(b, p)
        else:
            vg(b, p)
        dinc(b, V, "upd", 1)
        return cnt(b, "upd")

    def emit_fixup_masks(units8):
        ps = []
        for u in units8:
            ps.append(V.scalar_tensor_tensor(out=s16ap(u), in0=s16ap(u),
                                             scalar=b.st_hi[:, u:u + 1], in1=s16ap(u),
                                             op0=OP.is_lt, op1=OP.mult))
        vg(b, ps[-1])
        for u in units8:
            mx = V.max(out=b.w9[:, u * 9:u * 9 + 8], in_=s16ap(u))
            b.sfree_post[b.guctr + u] = inc(b, mx, "sfree", 1)
            V.tensor_copy(b.w9[:, u * 9 + 8:u * 9 + 9], b.st_lo[:, u:u + 1])

    def emit_fixup_picks(units8):
        u0s = units8[0]
        Lp = lambda t: t[:, u0s:u0s + 2 * GW]
        scr = lambda k: b.w9[:, 288 + k * 9:288 + k * 9 + 9]
        p = V.tensor_scalar(out=Lp(b.st_s1), in0=Lp(b.st_chi), scalar1=-1.0, scalar2=203.0,
                            op0=OP.mult, op1=OP.add)
        vg(b, p)
        V.tensor_scalar(out=Lp(b.st_u2), in0=Lp(b.st_s1), scalar1=1.0, scalar2=None,
                        op0=OP.add)
        for i, u in enumerate(units8):
            p = V.tensor_scalar(out=scr(2 * i), in0=b.iota9[:],
                                scalar1=b.st_s1[:, u:u + 1], scalar2=None, op0=OP.is_equal)
        vg(b, p)
        for i, u in enumerate(units8):
            V.tensor_scalar(out=scr(2 * i + 1), in0=b.iota9[:],
                            scalar1=b.st_u2[:, u:u + 1], scalar2=None, op0=OP.is_equal)
            p = V.tensor_tensor(out=scr(2 * i), in0=b.w9[:, u * 9:u * 9 + 9],
                                in1=scr(2 * i), op=OP.mult)
        vg(b, p)
        for i, u in enumerate(units8):
            V.tensor_tensor(out=scr(2 * i + 1), in0=b.w9[:, u * 9:u * 9 + 9],
                            in1=scr(2 * i + 1), op=OP.mult)
            p = V.tensor_scalar(out=scr(2 * i), in0=scr(2 * i), scalar1=0.0, scalar2=0.0,
                                op0=OP.add, op1=OP.add, accum_out=b.st_s2[:, u:u + 1])
        vg(b, p)
        for i, u in enumerate(units8):
            p = V.tensor_scalar(out=scr(2 * i + 1), in0=scr(2 * i + 1), scalar1=0.0,
                                scalar2=0.0, op0=OP.add, op1=OP.add,
                                accum_out=b.st_s3[:, u:u + 1])
        vg(b, p)
        V.tensor_tensor(out=Lp(b.st_s2), in0=Lp(b.st_s2), in1=Lp(b.st_s3), op=OP.add)
        V.tensor_scalar(out=Lp(b.st_u1), in0=Lp(b.st_s1), scalar1=6.5, scalar2=None,
                        op0=OP.is_le)
        V.tensor_scalar(out=Lp(b.st_sm), in0=Lp(b.st_s1), scalar1=6.5, scalar2=None,
                        op0=OP.is_gt)
        p = V.tensor_scalar(out=Lp(b.st_sq), in0=Lp(b.st_clo), scalar1=204.0, scalar2=None,
                            op0=OP.is_equal)
        vg(b, p)
        V.tensor_scalar(out=Lp(b.st_s2), in0=Lp(b.st_s2), scalar1=0.5, scalar2=None,
                        op0=OP.mult)
        p = V.tensor_tensor(out=Lp(b.st_sm), in0=Lp(b.st_sm), in1=Lp(b.st_sq), op=OP.mult)
        vg(b, p)
        V.tensor_tensor(out=Lp(b.st_u1), in0=Lp(b.st_u1), in1=Lp(b.st_sm), op=OP.add)
        p = V.tensor_scalar(out=Lp(b.st_sd), in0=Lp(b.st_s1), scalar1=-0.5, scalar2=None,
                            op0=OP.is_ge)
        vg(b, p)
        V.tensor_tensor(out=Lp(b.st_u1), in0=Lp(b.st_u1), in1=Lp(b.st_sd), op=OP.mult)
        p = V.tensor_copy(Lp(b.st_ts), Lp(b.st_lo))
        vg(b, p)
        p = V.copy_predicated(out=Lp(b.st_ts), mask=Lp(b.st_u1).bitcast(I32),
                              data=Lp(b.st_s2))
        vg(b, p)

    upd0 = cnt(b, "upd")
    for gp in range(32 // (2 * GW)):
        for j in range(gp * GW, (gp + 1) * GW):
            wt(b, T, "aev", cnt(b, "aev"))
            for blk in range(NBLK):
                mm = T.matmul(b.psA[:, blk * 512:(blk + 1) * 512],
                              qsl(hA, j * 128, 128), ksl(hA, blk * 512, 512),
                              start=True, stop=True, tile_position=(0, 0))
                mm = T.matmul(b.psB[:, blk * 512:(blk + 1) * 512],
                              qsl(hB, j * 128, 128), ksl(hB, blk * 512, 512),
                              start=True, stop=True, tile_position=(64, 0))
            ntm = inc(b, mm, "tmm", 1)
            wt(b, A, "tmm", ntm)
            for hi_, ps in ((0, b.psA), (1, b.psB)):
                u = 2 * j + hi_
                prev = b.sfree_post.get(b.guctr + u - SRING)
                if prev is not None:
                    wt(b, A, "sfree", prev)
                ev = A.activation(out=s16ap(u), in_=ps[:], func=AF.Copy, scale=0.125)
                evposts[u] = inc(b, ev, "aev", 1)
                sqp = A.activation(out=xgsl(b, XG_E, 2048), in_=ps[:], func=AF.Square,
                                   scale=0.125, accum_out=b.st_sq[:, u:u + 1])
                inc(b, sqp, "aev", 1)
        u0s = 2 * gp * GW
        Ls = lambda t: t[:, u0s:u0s + 2 * GW]
        wt(b, V, "aev", cnt(b, "aev"))
        p = V.tensor_scalar(out=Ls(b.st_sq), in0=Ls(b.st_sq), scalar1=1.0 / S, scalar2=None,
                            op0=OP.mult)
        vg(b, p)
        nvv = dinc(b, V, "vev", 1)
        wt(b, A, "vev", nvv)
        sq = A.activation(out=Ls(b.st_sd), in_=Ls(b.st_sq), func=AF.Sqrt)
        nav = inc(b, sq, "aev", 1)
        wt(b, V, "aev", nav)
        p = V.tensor_scalar(out=Ls(b.st_pr), in0=Ls(b.st_sd), scalar1=ZINIT, scalar2=None,
                            op0=OP.mult)
        vg(b, p)
        upd0gp = dinc(b, V, "upd", 1)
        halves = [list(range((2 * gp + h) * GW, (2 * gp + h + 1) * GW)) for h in (0, 1)]
        upd_post = [upd0gp, upd0gp]
        for r in range(RPROBE):
            npr0 = emit_probes(halves[0], r, upd_post[0])
            npr1 = emit_probes(halves[1], r, upd_post[1])
            upd_post[0] = emit_update(halves[0], r, npr0)
            upd_post[1] = emit_update(halves[1], r, npr1)
        emit_fixup_masks(halves[0] + halves[1])
        emit_fixup_picks(halves[0] + halves[1])

    b.guctr += 32
    npick = dinc(b, V, "ethx", 1)
    if b.debug:
        wt(b, Y, "ethx", npick)
        dump(b, b.st_ts[:], b.dbg_ts[bi, pair])

    import os as _os
    if _os.environ.get("KNOP2", "0") == "1":
        return
    wt(b, A, "ethx", npick)
    A.activation(out=b.eth[:].rearrange("p (two j) -> p two j", j=16),
                 in_=b.st_ts[:].rearrange("p (j two) -> p two j", two=2), func=AF.Exp)
    nav = dinc(b, A, "ethx", 1)
    wt(b, T, "ethx", cnt(b, "ethx"))
    wt(b, T, "aev", cnt(b, "aev"))
    tr = T.transpose(b.psB[0:32, 0:128], b.eth[:], b.identf[:])
    ntm = inc(b, tr, "tmm", 1)
    wt(b, A, "tmm", ntm)
    ev = A.activation(out=b.ethT[:], in_=b.psB[0:32, 0:128], func=AF.Copy)
    nav = inc(b, ev, "aev", 1)
    wt(b, Y, "aev", nav)
    for hi_ in range(2):
        d = Y.dma_start(b.ethd[hi_:hi_ + 1, :].rearrange("a (t q) -> (a t) q", q=128),
                        b.ethT[hi_ * 16:(hi_ + 1) * 16, :])
        inc(b, d, "d_eth", 16)
    wt(b, G, "d_eth", cnt(b, "d_eth"))
    for hi_ in range(2):
        d = G.dma_start(xgsl(b, XG_ETH + hi_ * 2048, 2048),
                        b.ethd[hi_:hi_ + 1, :].to_broadcast((128, 2048)))
        inc(b, d, "d_bce", 16)
    nbc = cnt(b, "d_bce")
    if _os.environ.get("KNOPV", "0") == "1":
        return

    wt(b, T, "aev", cnt(b, "aev"))
    for hi_, h in ((0, hA), (1, hB)):
        ethbc = xgsl(b, XG_ETH + hi_ * 2048, 2048)
        for c in range(NT):
            es = c % 2
            eap = xgsl(b, XG_E + es * 2048, 2048)
            emap = xgsl(b, XG_EM, 2048)
            gap = xgsl(b, XG_G + es * 2048, 2048)
            indap = xgsl(b, XG_IND, 2048)
            if c > 0 or hi_ > 0:
                wt(b, T, "p2e", cnt(b, "p2e"))
            for blk in range(NBLK):
                mm = T.matmul(b.psA[:, blk * 512:(blk + 1) * 512],
                              ksl(h, c * 128, 128), qsl(h, blk * 512, 512),
                              start=True, stop=True,
                              tile_position=((h % 2) * 64, 0))
            ntm = inc(b, mm, "tmm", 1)
            wt(b, A, "tmm", ntm)
            if c >= 2:
                wt(b, A, "p2g", cnt(b, "p2g") - 1)
            ev = A.activation(out=eap, in_=b.psA[:], func=AF.Exp, scale=0.125)
            nev = inc(b, ev, "p2e", 1)
            wt(b, V, "p2e", nev)
            if hi_ == 0 and c == 0:
                wt(b, V, "d_bce", nbc)
            V.tensor_scalar(out=emap, in0=eap, scalar1=-1.0, scalar2=None, op0=OP.add)
            p = V.tensor_tensor(out=indap, in0=eap, in1=ethbc, op=OP.is_ge)
            vg(b, p)
            g16 = V.tensor_tensor(out=gap, in0=emap, in1=indap, op=OP.mult)
            ngg = inc(b, g16, "p2g", 1)
            wt(b, T, "p2g", ngg)
            for blk in range(NBLK):
                mm = T.matmul(b.psB[0:65, blk * 512:(blk + 1) * 512],
                              b.vb[:, c * 1040 + h * 65:c * 1040 + h * 65 + 65],
                              gap[:, blk * 512:(blk + 1) * 512],
                              start=(c == 0), stop=(c == NT - 1))
            inc(b, mm, "tmm", 1)
        if _os.environ.get("KNOOC", "0") == "1":
            continue
        npv = cnt(b, "tmm")
        wt(b, V, "tmm", npv)
        wt(b, T, "p2e", cnt(b, "p2e"))
        for blk in range(NBLK):
            if blk > 0:
                wt(b, V, "tmm", cnt(b, "tmm"))
            zst = b.o1[0:1, 0:512]
            p = V.tensor_scalar(out=zst, in0=b.psB[64:65, blk * 512:(blk + 1) * 512],
                                scalar1=float(S), scalar2=None, op0=OP.add)
            vg(b, p)
            p = V.reciprocal(out=zst, in_=zst)
            vg(b, p)
            p = V.tensor_copy(b.rzb[0:1, 0:512], zst)
            vg(b, p)
            nvv = dinc(b, V, "vev", 1)
            wt(b, T, "vev", nvv)
            mm = T.matmul(b.psA[0:64, blk * 512:(blk + 1) * 512], b.onesr[0:1, 0:64],
                          b.rzb[0:1, 0:512], start=True, stop=True)
            ntm = inc(b, mm, "tmm", 1)
            wt(b, V, "tmm", ntm)
            p = V.tensor_scalar(out=b.o1[:], in0=b.psB[0:64, blk * 512:(blk + 1) * 512],
                                scalar1=b.vsums[0:64, h:h + 1], scalar2=None, op0=OP.add)
            vg(b, p)
            p0 = (h % 2) * 64
            onap = (b.xf[p0:p0 + 64,
                         (h // 2) * 1024 + blk * 256:(h // 2) * 1024 + blk * 256 + 256]
                    .bitcast(BF16))
            v = V.tensor_tensor(out=onap, in0=b.o1[:],
                                in1=b.psA[0:64, blk * 512:(blk + 1) * 512], op=OP.mult)
            inc(b, v, "p2e", 1)


def p5_router(b, bi):
    nc = b.nc
    T, V, A, G, Y = nc.tensor, nc.vector, nc.scalar, nc.gpsimd, nc.sync
    barrier(b)
    h2 = lambda c: b.qb[:, c * 2048:(c + 1) * 2048]
    aevposts = []
    for tt in range(NT):
        bank = tt % 4
        ps = b.psA[:, bank * 512:bank * 512 + 8]
        if tt >= 4:
            wt(b, T, "aev", aevposts[tt - 4])
        for kc in range(NCH):
            mm = T.matmul(ps, h2(kc)[:, tt * 128:(tt + 1) * 128], b.rwtb[:, kc * 8:(kc + 1) * 8],
                          start=(kc == 0), stop=False)
        mm = T.matmul(ps, b.onesr[0:1, 0:128], b.rbrow[0:1, :], start=False, stop=True)
        ntm = inc(b, mm, "tmm", 1)
        wt(b, V, "tmm", ntm)
        p = V.tensor_reduce(out=b.prsc[:, 0:1], in_=ps, axis=mybir.AxisListType.X, op=OP.max)
        vg(b, p)
        p = V.tensor_scalar(out=b.prsc[:, 1:2], in0=b.prsc[:, 0:1], scalar1=-1.0,
                            scalar2=None, op0=OP.mult)
        vg(b, p)
        nvv = dinc(b, V, "vev", 1)
        wt(b, A, "vev", nvv)
        ev = A.activation(out=b.prsc[:, 4:12], in_=ps, func=AF.Exp,
                          bias=b.prsc[:, 1:2], scale=1.0, accum_out=b.prsc[:, 2:3])
        nav = inc(b, ev, "aev", 1)
        aevposts.append(nav)
        wt(b, V, "aev", nav)
        p = V.reciprocal(out=b.prsc[:, 3:4], in_=b.prsc[:, 2:3])
        vg(b, p)
        p = V.tensor_scalar(out=b.prsc[:, 4:12], in0=b.prsc[:, 4:12], scalar1=b.prsc[:, 3:4],
                            scalar2=None, op0=OP.mult)
        vg(b, p)
        p = V.max(out=b.prsc[:, 12:20], in_=b.prsc[:, 4:12])
        vg(b, p)
        st = V.scalar_tensor_tensor(out=b.wall[:, tt * 8:(tt + 1) * 8], in0=b.prsc[:, 4:12],
                                    scalar=b.prsc[:, 13:14], in1=b.prsc[:, 4:12],
                                    op0=OP.is_ge, op1=OP.mult)
        inc(b, st, "vev", 2)
    if b.debug:
        wt(b, Y, "vev")
        dump(b, b.wall[:], b.dbg_w)


def p6_ffn(b, bi):
    nc = b.nc
    T, V, A, G, Y = nc.tensor, nc.vector, nc.scalar, nc.gpsimd, nc.sync
    barrier(b)
    h2 = lambda c: b.qb[:, c * 2048:(c + 1) * 2048]
    w1b = b.kb[:, 0:16384]
    w2b = b.vb[:, 0:16384]
    h1b = b.sr[:, 0:8192]
    mf = lambda tt: mixf(b, tt)

    for e in range(-1, NE):
        if e == -1:
            a1, b1s, a2, b2s = b.sw1, b.sb1.rearrange("(c p) -> p c", p=128), \
                b.sw2, b.sb2[None, :]
        else:
            a1, b1s, a2, b2s = b.rw1[e], b.rb1[e].rearrange("(c p) -> p c", p=128), \
                b.rw2[e], b.rb2[e][None, :]
        wt(b, G, "ff1", cnt(b, "ff1"))
        wt(b, G, "ffg", cnt(b, "ffg"))
        _d, _s = ld3(w1b, a1, FH)
        d = G.dma_start(_d, _s)
        inc(b, d, "d_w1", 16)
        d = G.dma_start(b.b1ch[:, 0:16], b1s, allow_slow_non_contiguous=True)
        ndw1 = inc(b, d, "d_w1", 16)
        wt(b, G, "ff2", cnt(b, "ff2"))
        wt(b, G, "ffm", cnt(b, "ffm"))
        _d, _s = ld3(w2b, a2, D)
        d = G.dma_start(_d, _s)
        inc(b, d, "d_w2", 16)
        d = G.dma_start(b.b2row[0:1, :], b2s)
        ndw = inc(b, d, "d_w2", 16)
        for tb in range(4):
            for oc in range(16):
                i = oc % 2
                ps = (b.psA if i == 0 else b.psB)[:, 0:512]
                if oc >= 2:
                    wt(b, T, "ffg", cnt(b, "ffg") - 1)
                wt(b, T, "d_w1", ndw1)
                last1 = None
                for kc in range(NCH):
                    mm = T.matmul(ps, w1b[:, kc * 2048 + oc * 128:kc * 2048 + oc * 128 + 128],
                                  h2(kc)[:, tb * 512:(tb + 1) * 512],
                                  start=(kc == 0), stop=(kc == NCH - 1))
                ntm = inc(b, mm, "tmm", 1)
                if tb == 3 and oc == 15:
                    dinc(b, T, "ff1", 1)
                wt(b, A, "tmm", ntm)
                ev = A.activation(out=h1b[:, oc * 512:(oc + 1) * 512], in_=ps, func=AF.Gelu,
                                  bias=b.b1ch[:, oc:oc + 1])
                inc(b, ev, "ffg", 1)
            ng = cnt(b, "ffg")
            wt(b, T, "ffg", ng)
            wt(b, T, "d_w2", ndw)
            for ts_ in range(4):
                tt = tb * 4 + ts_
                wcol = b.wall[:, tt * 8 + e:tt * 8 + e + 1] if e >= 0 else None
                for dh in range(2):
                    i = (ts_ * 2 + dh) % 2
                    ps = (b.psA if i == 0 else b.psB)[:, 512:1024]
                    if ts_ * 2 + dh >= 2:
                        wt(b, T, "ffm", cnt(b, "ffm") - 1)
                    for oc in range(16):
                        mm = T.matmul(ps, h1b[:, oc * 512 + ts_ * 128:oc * 512 + ts_ * 128 + 128],
                                      w2b[:, oc * 1024 + dh * 512:oc * 1024 + dh * 512 + 512],
                                      start=(oc == 0), stop=False)
                    mm = T.matmul(ps, b.onesr[0:1, 0:128],
                                  b.b2row[0:1, dh * 512:(dh + 1) * 512], start=False, stop=True)
                    ntm = inc(b, mm, "tmm", 1)
                    if tb == 3 and ts_ == 3 and dh == 1:
                        dinc(b, T, "ff2", 1)
                    wt(b, V, "tmm", ntm)
                    dst = mf(tt)[:, dh * 512:(dh + 1) * 512]
                    if e == -1:
                        v = V.tensor_scalar(out=dst, in0=ps, scalar1=1.0, scalar2=None,
                                            op0=OP.mult)
                    else:
                        v = V.scalar_tensor_tensor(out=dst, in0=ps, scalar=wcol, in1=dst,
                                                   op0=OP.mult, op1=OP.add)
                    inc(b, v, "ffm", 1)
        if b.debug and e in (-1, NE - 1):
            dbg = b.dbg_mx0 if e == -1 else b.dbg_mx1
            wt(b, Y, "ffm", cnt(b, "ffm"))
            for tt in range(NT):
                d = Y.dma_start(dbg[tt], mf(tt))
                inc(b, d, "d_out", 16)
            wt(b, Y, "d_out")


def p7_out(b, bi):
    nc = b.nc
    T, V, A, G, Y = nc.tensor, nc.vector, nc.scalar, nc.gpsimd, nc.sync
    barrier(b)
    d = Y.dma_start(gbbc(b), b.rowd[3:4, 0:1024].to_broadcast((128, 1024)))
    inc(b, d, "d_bc", 16)
    ldposts = []
    trposts = []
    vposts = []
    for tt in range(NT):
        sslot = (tt % 2) * 1024
        if tt >= 2:
            wt(b, Y, "tmm", trposts[tt - 2])
        x2sem = "d_x2a" if (tt % 2) == 0 else "d_x2b"
        for c in range(NCH):
            d = Y.dma_start(srf(b, sslot + c * 128, 128),
                            b.spillB[c][:, tt * 128:(tt + 1) * 128])
            inc(b, d, x2sem, 16)
        ldposts.append(cnt(b, x2sem))
        wt(b, T, x2sem, ldposts[tt])
        ps = b.psA if (tt % 2) == 0 else b.psB
        if tt >= 2:
            wt(b, T, "vev", vposts[tt - 2])
        for c in range(NCH):
            tr = T.transpose(ps[:, c * 128:(c + 1) * 128],
                             srf(b, sslot + c * 128, 128), b.identf[:])
            inc(b, tr, "tmm", 1)
        trposts.append(cnt(b, "tmm"))
        wt(b, V, "tmm", trposts[tt])
        if tt == 0:
            wt(b, V, "d_bc", cnt(b, "d_bc"))
        p = V.tensor_tensor(out=otsv(b), in0=mixf(b, tt), in1=gbbc(b),
                            op=OP.mult)
        vg(b, p)
        v = V.tensor_tensor(out=otsv(b), in0=otsv(b), in1=ps[:, 0:1024],
                            op=OP.add)
        vposts.append(inc(b, v, "vev", 1))
        wt(b, Y, "vev", vposts[tt])
        d = Y.dma_start(b.out[bi, tt * 128:(tt + 1) * 128, :], otsv(b))
        inc(b, d, "d_out", 16)


def kernel(**inputs):
    xs = np.ascontiguousarray(inputs["L0_attn_x"])
    cs = np.ascontiguousarray(inputs["L0_c"])
    names = {
        "wq": "L0_attn_Wq", "wk": "L0_attn_Wk", "wv": "L0_attn_Wv", "wo": "L0_attn_Wo",
        "bq": "L0_attn_bq", "bk": "L0_attn_bk", "bv": "L0_attn_bv", "bo": "L0_attn_bo",
        "rwt": "L0_ffn_router_w", "rbt": "L0_ffn_router_b",
        "sw1": "L0_ffn_sw1", "sb1": "L0_ffn_sb1", "sw2": "L0_ffn_sw2", "sb2": "L0_ffn_sb2",
        "rw1": "L0_ffn_rw1", "rb1": "L0_ffn_rb1", "rw2": "L0_ffn_rw2", "rb2": "L0_ffn_rb2",
        "wmod": "L0_wmod",
    }
    shared = {k: np.ascontiguousarray(np.asarray(inputs[v], dtype=np.float32))
              for k, v in names.items()}
    b = build(debug=False)
    emit(b)
    in_maps = []
    for c in range(8):
        m = dict(shared)
        m["x"] = np.ascontiguousarray(xs[2 * c:2 * c + 2])
        m["cc"] = np.ascontiguousarray(cs[2 * c:2 * c + 2])
        in_maps.append(m)
    res = run_bass_kernel_spmd(b.nc, in_maps, core_ids=list(range(8)))
    out = np.concatenate([r["out"] for r in res.results], axis=0).astype(np.float32)
    b.ctx.close()
    return out
